# version 14
# baseline (speedup 1.0000x reference)
import contextlib
import numpy as np
import concourse.bass as bass
import concourse.mybir as mybir
import concourse.bass_utils as _bass_utils
from concourse.bass_utils import run_bass_kernel_spmd

LDW_OPT = True
_orig_bvo = _bass_utils.bir_verify_and_optimise


def _bvo_ldwopt(*a, **kw):
    orig_run = _bass_utils.run_command

    def run2(cmd, **k):
        if LDW_OPT:
            cmd = ["--enable-ldw-opt=true" if c == "--enable-ldw-opt=false" else c
                   for c in cmd]
        return orig_run(cmd, **k)

    _bass_utils.run_command = run2
    try:
        return _orig_bvo(*a, **kw)
    finally:
        _bass_utils.run_command = orig_run


_bass_utils.bir_verify_and_optimise = _bvo_ldwopt

F32 = mybir.dt.float32
BF16 = mybir.dt.bfloat16
AF = mybir.ActivationFunctionType
ALU = mybir.AluOpType
AX = mybir.AxisListType

NCORE = 8
D = 1024
T_FULL = 16384
TC = T_FULL // NCORE
KD = 8
EPS = 1e-5
FUSED = True
LAYERS_LIST = None
SKIP_CC = False
SCR_KB = 74
SKIP_PHASES = set()
ATTN1_MDTS = 32
ATTN1_NO_SCAN = False
ATTN1_NO_PROJ = False
ATTN1_NO_COMBINE = False
ATTN1_NO_AH = False
ATTN1_NO_SPILL = False
ATTN1_NO_PASS2 = False


class Ctx:
    def __init__(self, eng_name, eng, S, B):
        self.e = eng_name
        self.eng = eng
        self.S = S
        self.B = B
        self.c = {"T": 0, "V": 0, "S": 0, "E": 0, "G": 0}
        self.d = {}
        self.tok_mm = {}
        self.tok_ev = {}
        self.waited = {}

    def t(self):
        return self.e == "tensor"

    def v(self):
        return self.e == "vector"

    def s(self):
        return self.e == "scalar"

    def g(self):
        return self.e == "gpsimd"

    def y(self):
        return self.e == "sync"

    def bump_t(self, ins):
        self.c["T"] += 1
        if ins is not None:
            ins.then_inc(self.S["SEM_T"], 1)
        return self.c["T"]

    def dma(self, sem, ins=None):
        self.d[sem] = self.d.get(sem, 0) + 16
        if ins is not None:
            ins.then_inc(self.S[sem], 16)
        return self.d[sem]

    def wait(self, sem, val):
        if val > 0 and self.waited.get(sem, 0) < val:
            self.eng.wait_ge(self.S[sem], val)
            self.waited[sem] = val

    def wait_tok(self, tok):
        key, val = tok
        self.wait("SEM_" + key, val)

    def join(self):
        own = {"tensor": "T", "vector": "V", "scalar": "S", "gpsimd": "G"}.get(self.e)
        for k in ("T", "V", "S", "E", "G"):
            if k != own:
                self.wait("SEM_" + k, self.c[k])

    def prewait(self, banks):
        if self.t():
            best = {}
            for b in banks:
                tok = self.tok_ev.get(b)
                if tok:
                    key, val = tok
                    best[key] = max(best.get(key, 0), val)
            for key, val in best.items():
                self.wait("SEM_" + key, val)


def V(ctx, name, *args, waits=(), evac=False, needE=False, **kw):
    ins = None
    if ctx.v():
        for s, v in waits:
            ctx.wait(s, v)
        ctx.wait("SEM_V", ctx.c["V"])
        if evac or needE:
            ctx.wait("SEM_E", ctx.c["E"])
        ins = getattr(ctx.eng, name)(*args, **kw)
    key = "E" if evac else "V"
    ctx.c[key] += 1
    if ins is not None:
        ins.then_inc(ctx.S["SEM_" + key], 1)
    return ctx.c[key]


def SS(ctx, name, *args, waits=(), evac=False, needE=False, **kw):
    ins = None
    if ctx.s():
        for s, v in waits:
            ctx.wait(s, v)
        ctx.wait("SEM_S", ctx.c["S"])
        if evac or needE:
            ctx.wait("SEM_E", ctx.c["E"])
        ins = getattr(ctx.eng, name)(*args, **kw)
    key = "E" if evac else "S"
    ctx.c[key] += 1
    if ins is not None:
        ins.then_inc(ctx.S["SEM_" + key], 1)
    return ctx.c[key]


def G(ctx, name, *args, waits=(), **kw):
    ins = None
    if ctx.g():
        for s, v in waits:
            ctx.wait(s, v)
        ctx.wait("SEM_G", ctx.c["G"])
        ins = getattr(ctx.eng, name)(*args, **kw)
    ctx.c["G"] += 1
    if ins is not None:
        ins.then_inc(ctx.S["SEM_G"], 1)
    return ctx.c["G"]


class Slab:
    def __init__(self, sem, free_tok=None):
        self.sem = sem
        self.loaded = 0
        self.free_tok = free_tok


def slab_load(ctx, slab, dst, src):
    ins = None
    if ctx.g():
        if slab.free_tok:
            ctx.wait("SEM_T", slab.free_tok)
        ins = ctx.eng.dma_start(out=dst, in_=src)
    slab.loaded = ctx.dma(slab.sem, ins)


def slab_wait(ctx, slab):
    if ctx.t():
        ctx.wait(slab.sem, slab.loaded)


def slab_free(ctx, slab):
    slab.free_tok = ctx.c["T"]



def build_program(layers):
    nc = bass.Bass("TRN2", target_bir_lowering=False)
    B = {}

    def di(name, shape):
        B[name] = nc.dram_tensor(name, shape, F32, kind="ExternalInput").ap()

    di("x", [D, TC])
    di("cones", [128, 128])
    di("cident", [128, 128])
    di("cematb", [16, 2048])
    if 0 in layers:
        for w in ("wq", "wk", "wv", "wg", "wo"):
            di(w, [D, D])
        di("ld0", [1, 1])
        di("cdiff", [128, 128])
        di("clt", [128, 128])
        di("esel", [D, 16])
        di("keys", [16, D, 128])
        di("values", [16, 128, D])
    if 1 in layers:
        for w in ("wga", "wva", "wda"):
            di(w, [4, D, D])
        for b in ("bga", "bva", "bda"):
            di(b, [128, 32])
        di("gw", [D, 4])
        di("gfw", [D, 8])
        di("gfb", [1, 8])
        di("efb", [1, 8])
        di("w1", [8, D, 4096])
        di("b1c", [128, 256])
        di("w2", [8, 4096, D])
        di("b2c", [128, 64])
        di("ce4", [4, 512])
        di("cmsel", [128, 8])
        di("cmselinv", [128, 8])
        B["sp_h"] = nc.dram_tensor("sp_h", [32, 128, TC], F32).ap()
        B["sp_p"] = nc.dram_tensor("sp_p", [32, 128, TC], F32).ap()
        B["cc_in"] = nc.dram_tensor("cc_in", [128, 64], F32)
        B["cc_out"] = nc.dram_tensor("cc_out", [NCORE * 128, 64], F32)
    B["y"] = nc.dram_tensor("y", [D, TC], F32, kind="ExternalOutput").ap()

    with contextlib.ExitStack() as es:
        e = es.enter_context
        xr = e(nc.sbuf_tensor("xr", [128, KD, TC], F32))
        xn = e(nc.sbuf_tensor("xn", [128, KD, TC], BF16))
        WST = e(nc.sbuf_tensor("WST", [128, 12288], BF16))
        SCR = e(nc.sbuf_tensor("SCR", [128, SCR_KB * 512], BF16))
        ones_f = e(nc.sbuf_tensor("ones_f", [128, 128], F32))
        id_f = e(nc.sbuf_tensor("id_f", [128, 128], F32))
        ematb = e(nc.sbuf_tensor("ematb", [16, 2048], BF16))
        B.update(xr=xr, xn=xn, WST=WST, SCR=SCR, SCRF=SCR.bitcast(F32),
                 ones_f=ones_f, id_f=id_f, ematb=ematb)
        if 0 in layers:
            B["diff_f"] = e(nc.sbuf_tensor("diff_f", [128, 128], F32))
            B["lt_f"] = e(nc.sbuf_tensor("lt_f", [128, 128], F32))
            B["maskT"] = e(nc.sbuf_tensor("maskT", [128, 128], F32))
            B["esel_b"] = e(nc.sbuf_tensor("esel_b", [128, KD, 16], BF16))
            B["ld0_f"] = e(nc.sbuf_tensor("ld0_f", [1, 1], F32))
            B["lnd_col"] = e(nc.sbuf_tensor("lnd_col", [128, 1], F32))
        if 1 in layers:
            B["gw_b"] = e(nc.sbuf_tensor("gw_b", [128, KD, 4], BF16))
            B["gfw_b"] = e(nc.sbuf_tensor("gfw_b", [128, KD, 8], BF16))
            for b in ("bga", "bva", "bda"):
                B[b + "_f"] = e(nc.sbuf_tensor(b + "_f", [128, 32], F32))
            B["b1c_f"] = e(nc.sbuf_tensor("b1c_f", [128, 256], F32))
            B["b2c_f"] = e(nc.sbuf_tensor("b2c_f", [128, 64], F32))
            B["e4f"] = e(nc.sbuf_tensor("e4f", [4, 512], F32))
            B["msel_f"] = e(nc.sbuf_tensor("msel_f", [128, 8], F32))
            B["mselinv_f"] = e(nc.sbuf_tensor("mselinv_f", [128, 8], F32))
            B["gfb_f"] = e(nc.sbuf_tensor("gfb_f", [1, 8], F32))
            B["efb_f"] = e(nc.sbuf_tensor("efb_f", [1, 8], F32))
            B["b8_f"] = e(nc.sbuf_tensor("b8_f", [1, 8], F32))
            B["bias8_b"] = e(nc.sbuf_tensor("bias8_b", [128, 8], F32))
            B["exch"] = e(nc.sbuf_tensor("exch", [128, 64], F32))
            B["AH"] = e(nc.sbuf_tensor("AH", [128, NCORE, 64], F32))
            B["carry_f"] = e(nc.sbuf_tensor("carry_f", [128, 32], F32))
        pb = [e(nc.psum_tensor(f"pb{i}", [128, 512], F32)) for i in range(4)]
        pO = [e(nc.psum_tensor(f"pO{i}", [128, 1024], F32)) for i in range(2)]
        B["pb"] = pb
        B["pO"] = pO

        sem_names = (["SEM_T", "SEM_V", "SEM_S", "SEM_E", "SEM_G", "dXL", "dCL", "dVL", "dSP",
                      "dRL0", "dRL1", "dCC", "cCC", "dST"]
                     + [f"dSL{i}" for i in range(6)])
        S = {n: e(nc.semaphore(n)) for n in sem_names}
        block = e(nc.Block())

        def run(eng_name):
            def f(eng):
                ctx = Ctx(eng_name, eng, S, B)
                emit_program(ctx, layers)
            return f

        block.tensor(run("tensor"))
        block.vector(run("vector"))
        block.scalar(run("scalar"))
        block.gpsimd(run("gpsimd"))
        block.sync(run("sync"))

    return nc



def _view(ap, shape):
    if len(shape) > 2:
        names = " ".join(f"d{i}" for i in range(1, len(shape)))
        ap = ap.rearrange(f"p ({names}) -> p " + names,
                          **{f"d{i}": shape[i] for i in range(1, len(shape) - 1)})
    return ap


def scr_bf(B, kb, shape):
    off = int(kb * 512)
    n = int(np.prod(shape[1:]))
    assert off + n <= SCR_KB * 512, (kb, shape)
    return _view(B["SCR"][0:shape[0], off:off + n], shape)


def scr_f32(B, kb, shape):
    off = int(kb * 256)
    n = int(np.prod(shape[1:]))
    assert off + n <= SCR_KB * 256, (kb, shape)
    return _view(B["SCRF"][0:shape[0], off:off + n], shape)


def wst_bf(B, kb, shape):
    off = int(kb * 512)
    n = int(np.prod(shape[1:]))
    assert off + n <= 12288, (kb, shape)
    return _view(B["WST"][0:shape[0], off:off + n], shape)



def ph_preload(ctx, layers):
    B = ctx.B
    for i in range(KD):
        ins = None
        if ctx.y():
            ins = ctx.eng.dma_start(out=B["xr"][:, i, :],
                                    in_=B["x"][128 * i:128 * (i + 1), :])
        ctx.dma("dXL", ins)
    consts = [("cones", "ones_f"), ("cident", "id_f")]
    if 0 in layers:
        consts += [("cdiff", "diff_f"), ("clt", "lt_f"), ("ld0", "ld0_f")]
    if 1 in layers:
        consts += [("gfb", "gfb_f"), ("efb", "efb_f"), ("ce4", "e4f"),
                   ("cmsel", "msel_f"), ("cmselinv", "mselinv_f"),
                   ("bga", "bga_f"), ("bva", "bva_f"), ("bda", "bda_f"),
                   ("b1c", "b1c_f"), ("b2c", "b2c_f")]
    for src, dst in consts:
        ins = None
        if ctx.g():
            ins = ctx.eng.dma_start(out=B[dst][:], in_=B[src])
        ctx.dma("dCL", ins)
    casts = [(B["ematb"][:], "cematb", None)]
    if 0 in layers:
        casts.append((B["esel_b"][:], "esel",
                      B["esel"].rearrange("(k p) e -> p k e", p=128)))
    if 1 in layers:
        casts.append((B["gw_b"][:], "gw",
                      B["gw"].rearrange("(k p) e -> p k e", p=128)))
        casts.append((B["gfw_b"][:], "gfw",
                      B["gfw"].rearrange("(k p) e -> p k e", p=128)))
    for dst, src_name, src_ap in casts:
        ins = None
        if ctx.g():
            ins = ctx.eng.dma_start(out=dst, in_=B[src_name] if src_ap is None else src_ap)
        ctx.dma("dCL", ins)

    dcl = ctx.d["dCL"]
    if 0 in layers:
        SS(ctx, "activation", B["ld0_f"][:], B["ld0_f"][:], AF.Sigmoid,
           waits=[("dCL", dcl)])
        SS(ctx, "activation", B["ld0_f"][:], B["ld0_f"][:], AF.Ln)
        ins = None
        if ctx.t():
            ctx.wait("SEM_S", ctx.c["S"])
            ctx.wait("dCL", dcl)
            ins = ctx.eng.matmul(B["pb"][0][:, 0:1], B["ones_f"][0:1, :],
                                 B["ld0_f"][:], start=True, stop=True)
        ctx.tok_mm["b0"] = ctx.bump_t(ins)
        V(ctx, "tensor_copy", B["lnd_col"][:], B["pb"][0][:, 0:1],
          waits=[("SEM_T", ctx.tok_mm["b0"]), ("dCL", dcl)], evac=True)
        ctx.tok_ev["b0"] = ("E", ctx.c["E"])
        V(ctx, "tensor_scalar_mul", B["maskT"][:], B["diff_f"][:], B["lnd_col"][:], needE=True)
        SS(ctx, "activation", B["maskT"][:], B["maskT"][:], AF.Exp,
           waits=[("SEM_V", ctx.c["V"])])
        V(ctx, "tensor_mul", B["maskT"][:], B["maskT"][:], B["lt_f"][:],
          waits=[("SEM_S", ctx.c["S"])])

    if 1 in layers:
        V(ctx, "tensor_add", B["b8_f"][:], B["gfb_f"][:], B["efb_f"][:],
          waits=[("dCL", dcl)])
        ins = None
        if ctx.t():
            ctx.wait("SEM_V", ctx.c["V"])
            ctx.wait("dCL", dcl)
            ins = ctx.eng.matmul(B["pb"][1][:, 0:8], B["ones_f"][0:1, :],
                                 B["b8_f"][:], start=True, stop=True)
        ctx.tok_mm["b1"] = ctx.bump_t(ins)
        SS(ctx, "copy", B["bias8_b"][:], B["pb"][1][:, 0:8],
           waits=[("SEM_T", ctx.tok_mm["b1"])], evac=True)
        ctx.tok_ev["b1"] = ("E", ctx.c["E"])

    if ctx.e in ("tensor", "vector", "scalar"):
        ctx.wait("dXL", ctx.d["dXL"])
        ctx.wait("dCL", ctx.d["dCL"])


def ph_ln(ctx, add_resid):
    B = ctx.B
    sq = scr_f32(B, 0, [128, KD, 512])
    mean = scr_f32(B, 16, [128, 512])
    m2 = scr_f32(B, 18, [128, 512])
    var = scr_f32(B, 20, [128, 512])
    lnv = scr_f32(B, 22, [128, 512])
    rstd = scr_f32(B, 24, [128, 512])
    bterm = scr_f32(B, 26, [128, 512])
    xtmp = scr_f32(B, 28, [128, 512])
    xnf = [scr_f32(B, 30, [128, 512]), scr_f32(B, 32, [128, 512])]
    cast_tok = [0, 0]
    sq_tok = 0

    ctx.join()
    for tt in range(4):
        cols = slice(512 * tt, 512 * (tt + 1))
        for i in range(KD):
            SS(ctx, "activation", sq[:, i, :], B["xr"][:, i, cols], AF.Square,
               waits=([("SEM_T", sq_tok)] if i == 0 else []))
        ins = None
        if ctx.t():
            ctx.wait_tok(ctx.tok_ev.get("b0", ("V", 0)))
            for i in range(KD):
                ins = ctx.eng.matmul(B["pb"][0][:], B["ones_f"][:], B["xr"][:, i, cols],
                                     start=(i == 0), stop=(i == KD - 1))
        ctx.tok_mm["b0"] = ctx.bump_t(ins)
        ins = None
        if ctx.t():
            ctx.wait_tok(ctx.tok_ev.get("b1", ("V", 0)))
            ctx.wait("SEM_S", ctx.c["S"])
            for i in range(KD):
                ins = ctx.eng.matmul(B["pb"][1][:], B["ones_f"][:], sq[:, i, :],
                                     start=(i == 0), stop=(i == KD - 1))
        ctx.tok_mm["b1"] = ctx.bump_t(ins)
        sq_tok = ctx.c["T"]
        V(ctx, "tensor_scalar_mul", mean, B["pb"][0][:], 1.0 / D,
          waits=[("SEM_T", ctx.c["T"])], evac=True)
        ctx.tok_ev["b0"] = ("E", ctx.c["E"])
        V(ctx, "tensor_mul", m2, mean, mean, needE=True)
        V(ctx, "scalar_tensor_tensor", var, B["pb"][1][:], 1.0 / D, m2,
          ALU.mult, ALU.subtract, evac=True)
        ctx.tok_ev["b1"] = ("E", ctx.c["E"])
        V(ctx, "tensor_scalar_add", var, var, EPS, needE=True)
        SS(ctx, "activation", lnv, var, AF.Ln,
           waits=[("SEM_V", ctx.c["V"]), ("SEM_E", ctx.c["E"])])
        SS(ctx, "activation", rstd, lnv, AF.Exp, scale=-0.5)
        V(ctx, "scalar_tensor_tensor", bterm, mean, -1.0, rstd, ALU.mult, ALU.mult,
          waits=[("SEM_S", ctx.c["S"])], needE=True)
        for i in range(KD):
            slot = i % 2
            V(ctx, "tensor_mul", xtmp, B["xr"][:, i, cols], rstd,
              waits=[("SEM_S", cast_tok[slot])])
            V(ctx, "tensor_add", xnf[slot], xtmp, bterm)
            if add_resid:
                V(ctx, "tensor_add", B["xr"][:, i, cols], B["xr"][:, i, cols],
                  xnf[slot])
            SS(ctx, "copy", B["xn"][:, i, cols], xnf[slot],
               waits=[("SEM_V", ctx.c["V"])])
            cast_tok[slot] = ctx.c["S"]


def ph_attn0(ctx):
    B = ctx.B
    Qb = scr_bf(B, 0, [128, KD, 1024])
    Kb = scr_bf(B, 16, [128, KD, 1024])
    Vt = scr_bf(B, 32, [128, KD, 1024])
    Gb = scr_bf(B, 48, [128, KD, 1024])
    gO = scr_bf(B, 64, [128, KD, 512])
    Sb = [scr_bf(B, 72, [128, 128]), scr_bf(B, 72.5, [128, 128])]

    slabs = [Slab(f"dSL{i}", ctx.c["T"]) for i in range(3)]
    slab_view = [wst_bf(B, 8 * i, [128, KD, 512]) for i in range(3)]

    def wsrc(name, h):
        return B[name][:, 512 * h:512 * (h + 1)].rearrange("(k p) f -> p k f", p=128)

    seq = []
    for tg in range(2):
        for w in ("wq", "wk", "wv", "wg", "wo"):
            for h in range(2):
                seq.append((w, h))

    for j in range(2):
        slab_load(ctx, slabs[j % 3], slab_view[j % 3], wsrc(*seq[j]))
    nload = 2

    ctx.join()

    gO_wo_tok = 0
    for tg in range(2):
        for wi, w in enumerate(("wq", "wk", "wv", "wg")):
            for h in range(2):
                j = tg * 10 + wi * 2 + h
                slot = slabs[j % 3]
                sview = slab_view[j % 3]
                if nload < len(seq):
                    slab_load(ctx, slabs[nload % 3], slab_view[nload % 3], wsrc(*seq[nload]))
                    nload += 1
                slab_wait(ctx, slot)
                if w in ("wq", "wk", "wg"):
                    dst = {"wq": Qb, "wk": Kb, "wg": Gb}[w]
                    for i2 in range(4):
                        for tt2 in range(2):
                            bk = (i2 * 2 + tt2) % 2
                            bank = "b%d" % bk
                            pbk = B["pb"][bk]
                            ins = None
                            if ctx.t():
                                ctx.wait_tok(ctx.tok_ev.get(bank, ("V", 0)))
                                for k in range(KD):
                                    ins = ctx.eng.matmul(
                                        pbk[:],
                                        sview[:, k, 128 * i2:128 * (i2 + 1)],
                                        B["xn"][:, k, 1024 * tg + 512 * tt2:
                                                1024 * tg + 512 * (tt2 + 1)],
                                        start=(k == 0), stop=(k == KD - 1))
                            ctx.tok_mm[bank] = ctx.bump_t(ins)
                            out = dst[:, 4 * h + i2, 512 * tt2:512 * (tt2 + 1)]
                            if w == "wg":
                                SS(ctx, "activation", out, pbk[:], AF.Sigmoid,
                                   waits=[("SEM_T", ctx.tok_mm[bank])], evac=True)
                            else:
                                SS(ctx, "copy", out, pbk[:],
                                   waits=[("SEM_T", ctx.tok_mm[bank])], evac=True)
                            ctx.tok_ev[bank] = ("E", ctx.c["E"])
                else:
                    for ts in range(KD):
                        bank = "b%d" % (ts % 2)
                        pbk = B["pb"][ts % 2]
                        ins = None
                        if ctx.t():
                            ctx.wait_tok(ctx.tok_ev.get(bank, ("V", 0)))
                            for k in range(KD):
                                ins = ctx.eng.matmul(
                                    pbk[:],
                                    B["xn"][:, k, 1024 * tg + 128 * ts:
                                            1024 * tg + 128 * (ts + 1)],
                                    sview[:, k, :],
                                    start=(k == 0), stop=(k == KD - 1))
                        ctx.tok_mm[bank] = ctx.bump_t(ins)
                        SS(ctx, "copy", Vt[:, ts, 512 * h:512 * (h + 1)], pbk[:],
                           waits=[("SEM_T", ctx.tok_mm[bank])], evac=True)
                        ctx.tok_ev[bank] = ("E", ctx.c["E"])
                slab_free(ctx, slot)

        for grp in range(2):
            for ts4 in range(4):
                ts = grp * 4 + ts4
                sbank = "b%d" % (2 + ts % 2)
                pbs = B["pb"][2 + ts % 2]
                ins = None
                if ctx.t():
                    ctx.wait_tok(ctx.tok_ev.get(sbank, ("V", 0)))
                    if ts == 0:
                        ctx.wait("SEM_E", ctx.c["E"])
                        ctx.wait("SEM_S", ctx.c["S"])
                    for k in range(KD):
                        ins = ctx.eng.matmul(pbs[:, 0:128],
                                             Kb[:, k, 128 * ts:128 * (ts + 1)],
                                             Qb[:, k, 128 * ts:128 * (ts + 1)],
                                             start=(k == 0), stop=(k == KD - 1))
                ctx.tok_mm[sbank] = ctx.bump_t(ins)
                V(ctx, "tensor_mul", Sb[ts % 2], pbs[:, 0:128], B["maskT"][:],
                  waits=[("SEM_T", ctx.tok_mm[sbank])], evac=True)
                ctx.tok_ev[sbank] = ("E", ctx.c["E"])
                obank = "o%d" % (ts % 2)
                po = B["pO"][ts % 2]
                ins = None
                if ctx.t():
                    ctx.wait_tok(ctx.tok_ev.get(obank, ("V", 0)))
                    ctx.wait("SEM_E", ctx.c["E"])
                    for i in range(KD):
                        ins = ctx.eng.matmul(po[:, 128 * i:128 * (i + 1)],
                                             Vt[:, ts, 128 * i:128 * (i + 1)],
                                             Sb[ts % 2], start=True, stop=True)
                ctx.tok_mm[obank] = ctx.bump_t(ins)
                V(ctx, "tensor_mul",
                  gO[:, :, 128 * ts4:128 * (ts4 + 1)],
                  po[:].rearrange("p (i t) -> p i t", i=KD),
                  Gb[:, :, 128 * ts:128 * (ts + 1)],
                  waits=[("SEM_T", max(ctx.tok_mm[obank], gO_wo_tok))], evac=True)
                ctx.tok_ev[obank] = ("E", ctx.c["E"])
            gO_v_tok = ctx.c["E"]
            for h in range(2):
                j = tg * 10 + 8 + h
                slot = slabs[j % 3]
                sview = slab_view[j % 3]
                if nload < len(seq) and grp == 1:
                    slab_load(ctx, slabs[nload % 3], slab_view[nload % 3], wsrc(*seq[nload]))
                    nload += 1
                slab_wait(ctx, slot)
                for i2 in range(4):
                    bank = "b%d" % (i2 % 2)
                    ins = None
                    if ctx.t():
                        ctx.wait_tok(ctx.tok_ev.get(bank, ("V", 0)))
                        if i2 == 0 and h == 0:
                            ctx.wait("SEM_E", gO_v_tok)
                        for k in range(KD):
                            ins = ctx.eng.matmul(B["pb"][i2 % 2][:],
                                                 sview[:, k, 128 * i2:128 * (i2 + 1)],
                                                 gO[:, k, :],
                                                 start=(k == 0), stop=(k == KD - 1))
                    ctx.tok_mm[bank] = ctx.bump_t(ins)
                    out = B["xr"][:, 4 * h + i2,
                                  1024 * tg + 512 * grp:1024 * tg + 512 * (grp + 1)]
                    V(ctx, "tensor_add", out, out, B["pb"][i2 % 2][:],
                      waits=[("SEM_T", ctx.tok_mm[bank])], evac=True)
                    ctx.tok_ev[bank] = ("E", ctx.c["E"])
                if grp == 1:
                    slab_free(ctx, slot)
            gO_wo_tok = ctx.c["T"]


def ph_ffn0(ctx):
    B = ctx.B
    values_sb = scr_bf(B, 0, [128, 16, 1024])
    scores = scr_bf(B, 32, [128, 16, 512])
    sel = scr_bf(B, 48, [16, 2048])
    sctmp = [scr_bf(B, 52, [128, 512]), scr_bf(B, 53, [128, 512])]

    ctx.join()
    ins = None
    if ctx.g():
        ctx.wait("SEM_T", ctx.c["T"])
        ctx.wait("SEM_S", ctx.c["S"])
        ctx.wait("SEM_E", ctx.c["E"])
        ins = ctx.eng.dma_start(
            out=values_sb, in_=B["values"].rearrange("e p v -> p e v"))
    ctx.dma("dVL", ins)

    for tt in range(4):
        cols = slice(512 * tt, 512 * (tt + 1))
        ins = None
        if ctx.t():
            ctx.wait_tok(ctx.tok_ev.get("b2", ("V", 0)))
            for k in range(KD):
                ins = ctx.eng.matmul(B["pb"][2][0:16, :], B["esel_b"][:, k, :],
                                     B["xn"][:, k, cols],
                                     start=(k == 0), stop=(k == KD - 1))
        ctx.tok_mm["b2"] = ctx.bump_t(ins)
        SS(ctx, "activation", sel[:, cols], B["pb"][2][0:16, :], AF.Sigmoid,
           waits=[("SEM_T", ctx.tok_mm["b2"])], evac=True)
        ctx.tok_ev["b2"] = ("E", ctx.c["E"])

    kslabs = [Slab("dSL0", ctx.c["T"]), Slab("dSL1", ctx.c["T"])]
    kview = [wst_bf(B, 2 * i, [128, KD, 128]) for i in range(2)]

    def ksrc(e):
        return B["keys"][e].rearrange("(k p) h -> p k h", p=128)

    if ctx.e in ("tensor", "vector"):
        ctx.wait("dVL", ctx.d["dVL"])

    slab_load(ctx, kslabs[0], kview[0], ksrc(0))
    pB_tok = 0
    for tt in range(4):
        cols = slice(512 * tt, 512 * (tt + 1))
        for e in range(16):
            j = tt * 16 + e
            if j + 1 < 64:
                slab_load(ctx, kslabs[(j + 1) % 2], kview[(j + 1) % 2], ksrc((j + 1) % 16))
            slot = kslabs[j % 2]
            slab_wait(ctx, slot)
            bank = "b%d" % (e % 2)
            ins = None
            if ctx.t():
                ctx.wait_tok(ctx.tok_ev.get(bank, ("V", 0)))
                for k in range(KD):
                    ins = ctx.eng.matmul(B["pb"][e % 2][:], kview[j % 2][:, k, :],
                                         B["xn"][:, k, cols],
                                         start=(k == 0), stop=(k == KD - 1))
            ctx.tok_mm[bank] = ctx.bump_t(ins)
            slab_free(ctx, slot)
            ins = None
            if ctx.t():
                ctx.wait_tok(ctx.tok_ev.get("b3", ("V", 0)))
                if j == 0:
                    ctx.wait("SEM_E", ctx.c["E"])
                ins = ctx.eng.matmul(B["pb"][3][:], B["ematb"][:, 128 * e:128 * (e + 1)],
                                     sel[:, cols], start=True, stop=True)
            ctx.tok_mm["b3"] = ctx.bump_t(ins)
            SS(ctx, "activation", sctmp[e % 2], B["pb"][e % 2][:], AF.Relu,
               waits=[("SEM_T", ctx.tok_mm[bank]),
                      ("SEM_E", ctx.tok_ev.get("sct%d" % (e % 2), ("E", 0))[1])],
               evac=True)
            ctx.tok_ev[bank] = ("E", ctx.c["E"])
            wl = [("SEM_E", ctx.c["E"]), ("SEM_T", ctx.tok_mm["b3"])]
            if e == 0:
                wl.append(("SEM_T", pB_tok))
            V(ctx, "tensor_mul", scores[:, e, :], sctmp[e % 2], B["pb"][3][:],
              waits=wl, evac=True)
            ctx.tok_ev["b3"] = ("E", ctx.c["E"])
            ctx.tok_ev["sct%d" % (e % 2)] = ("E", ctx.c["E"])
        for v in range(KD):
            key = "o%d%d" % (v % 2, (v % 4) // 2)
            po = B["pO"][v % 2]
            half = (v % 4) // 2
            pseg = po[:, 512 * half:512 * (half + 1)]
            ins = None
            if ctx.t():
                ctx.wait_tok(ctx.tok_ev.get(key, ("V", 0)))
                if v == 0:
                    ctx.wait("SEM_E", ctx.c["E"])
                for ee in range(16):
                    ins = ctx.eng.matmul(pseg, values_sb[:, ee, 128 * v:128 * (v + 1)],
                                         scores[:, ee, :],
                                         start=(ee == 0), stop=(ee == 15))
            ctx.tok_mm[key] = ctx.bump_t(ins)
            out = B["xr"][:, v, cols]
            V(ctx, "tensor_add", out, out, pseg,
              waits=[("SEM_T", ctx.tok_mm[key])], evac=True)
            ctx.tok_ev[key] = ("E", ctx.c["E"])
        pB_tok = ctx.c["T"]


def ph_comb_gates(ctx):
    B = ctx.B
    comb_mM = scr_f32(B, 0, [4, 2048])
    glT = scr_f32(B, 8, [128, 64])
    gexT = scr_f32(B, 9, [128, 64])
    gkT = scr_f32(B, 10, [128, 64])
    gw8 = scr_f32(B, 11, [128, 128])
    comb_tm = scr_f32(B, 13, [128, 16, 4])
    gl_raw = scr_f32(B, 14, [128, 64])

    ctx.join()
    for ts in range(16):
        ins = None
        if ctx.t():
            if ts == 0:
                ctx.wait_tok(ctx.tok_ev.get("b3", ("V", 0)))
            for k in range(KD):
                ins = ctx.eng.matmul(B["pb"][3][:, 4 * ts:4 * (ts + 1)],
                                     B["xn"][:, k, 128 * ts:128 * (ts + 1)],
                                     B["gw_b"][:, k, :],
                                     start=(k == 0), stop=(k == KD - 1))
        ctx.tok_mm["b3"] = ctx.bump_t(ins)
    V(ctx, "tensor_copy", gl_raw, B["pb"][3][:, 0:64],
      waits=[("SEM_T", ctx.tok_mm["b3"])], evac=True)
    ctx.tok_ev["b3"] = ("E", ctx.c["E"])
    V(ctx, "tensor_copy", _view(glT, [128, 4, 16]),
      _view(gl_raw, [128, 16, 4]).rearrange("p a b -> p b a"), needE=True)
    cj = [glT[:, 16 * j:16 * (j + 1)] for j in range(4)]
    wj = [gw8[:, 16 * j:16 * (j + 1)] for j in range(8)]
    V(ctx, "tensor_max", wj[0], cj[0], cj[1])
    V(ctx, "tensor_tensor", wj[1], cj[0], cj[1], ALU.min)
    V(ctx, "tensor_max", wj[2], cj[2], cj[3])
    V(ctx, "tensor_tensor", wj[3], cj[2], cj[3], ALU.min)
    V(ctx, "tensor_tensor", wj[4], wj[0], wj[2], ALU.min)
    V(ctx, "tensor_max", wj[5], wj[1], wj[3])
    V(ctx, "tensor_max", wj[6], wj[4], wj[5])
    SS(ctx, "activation", gexT, glT, AF.Exp, waits=[("SEM_V", ctx.c["V"])])
    for j in range(4):
        V(ctx, "tensor_tensor", gkT[:, 16 * j:16 * (j + 1)], cj[j], wj[6], ALU.is_ge,
          waits=([("SEM_S", ctx.c["S"])] if j == 0 else []))
    V(ctx, "tensor_mul", gexT, gexT, gkT)
    V(ctx, "tensor_add", wj[7], gexT[:, 0:16], gexT[:, 16:32])
    V(ctx, "tensor_add", wj[7], wj[7], gexT[:, 32:48])
    V(ctx, "tensor_add", wj[7], wj[7], gexT[:, 48:64])
    V(ctx, "reciprocal", wj[0], wj[7])
    for j in range(4):
        V(ctx, "tensor_mul",
          comb_tm[:, :, j:j + 1].rearrange("p a b -> p (a b)"),
          gexT[:, 16 * j:16 * (j + 1)], wj[0])
    for q in range(4):
        ins = None
        if ctx.t():
            ctx.wait_tok(ctx.tok_ev.get("b2", ("V", 0)))
            ctx.wait("SEM_V", ctx.c["V"])
            for r in range(4):
                ts = 4 * q + r
                ins = ctx.eng.transpose(B["pb"][2][0:4, 128 * r:128 * (r + 1)],
                                        comb_tm[:, ts, :], B["id_f"][:])
        ctx.tok_mm["b2"] = ctx.bump_t(ins)
        SS(ctx, "copy", comb_mM[:, 512 * q:512 * (q + 1)], B["pb"][2][0:4, :],
           waits=[("SEM_T", ctx.tok_mm["b2"])], evac=True)
        ctx.tok_ev["b2"] = ("E", ctx.c["E"])


def ph_attn1(ctx):
    B = ctx.B
    comb_mM = scr_f32(B, 0, [4, 2048])
    comb_b = scr_f32(B, 8, [128, TC])
    xsp = [scr_f32(B, 16, [128, TC]), scr_f32(B, 8, [128, TC])]
    a_t = scr_f32(B, 24, [128, TC])
    hP = [(scr_f32(B, 32, [128, TC]), scr_f32(B, 40, [128, TC])),
          (scr_f32(B, 48, [128, TC]), scr_f32(B, 56, [128, TC]))]
    sgq2 = [scr_f32(B, 64, [128, 512]), scr_f32(B, 68, [128, 512])]
    tvq2 = [scr_f32(B, 66, [128, 512]), scr_f32(B, 70, [128, 512])]
    sgq = sgq2[0]

    slabs = [Slab(f"dSL{i}", ctx.c["T"]) for i in range(6)]
    sview = [wst_bf(B, 2 * i, [128, KD, 128]) for i in range(6)]

    def asrc(name, m, dt):
        return B[name][m][:, 128 * dt:128 * (dt + 1)].rearrange("(k p) f -> p k f", p=128)

    def load_mdt(j, m, dt):
        if ATTN1_NO_PROJ:
            return
        base = (j % 2) * 3
        for wi, name in enumerate(("wga", "wva", "wda")):
            slab_load(ctx, slabs[base + wi], sview[base + wi], asrc(name, m, dt))

    load_mdt(0, 0, 0)
    xs_free = [0, 0]
    tvq_tok = [0, 0]
    sgq_tok = [0, 0]

    def gen_comb_b(m, prev_use_tok):
        for q in range(4):
            ins = None
            if ctx.t():
                ctx.wait_tok(ctx.tok_ev.get("b2", ("V", 0)))
                ins = ctx.eng.matmul(B["pb"][2][:], B["e4f"][:, 128 * m:128 * (m + 1)],
                                     comb_mM[:, 512 * q:512 * (q + 1)],
                                     start=True, stop=True)
            ctx.tok_mm["b2"] = ctx.bump_t(ins)
            wl = [("SEM_T", ctx.tok_mm["b2"])]
            if q == 0:
                wl.append(("SEM_V", prev_use_tok))
            SS(ctx, "copy", comb_b[:, 512 * q:512 * (q + 1)], B["pb"][2][:], waits=wl,
               evac=True)
            ctx.tok_ev["b2"] = ("E", ctx.c["E"])

    for m in range(4):
        for dt in range(KD):
            j = m * KD + dt
            if j >= ATTN1_MDTS:
                continue
            base = (j % 2) * 3
            if j + 1 < min(32, ATTN1_MDTS):
                m2, dt2 = divmod(j + 1, KD)
                load_mdt(j + 1, m2, dt2)
            h_t, P_t = hP[j % 2]
            xs = xsp[j % 2]
            col = j
            for wi in range(0 if not ATTN1_NO_PROJ else 0, 3 if not ATTN1_NO_PROJ else 0):
                slot = slabs[base + wi]
                slab_wait(ctx, slot)
                for tt in range(4):
                    cols = slice(512 * tt, 512 * (tt + 1))
                    bank = "b%d" % tt
                    ins = None
                    if ctx.t():
                        ctx.wait_tok(ctx.tok_ev.get(bank, ("V", 0)))
                        for k in range(KD):
                            ins = ctx.eng.matmul(B["pb"][tt][:],
                                                 sview[base + wi][:, k, :],
                                                 B["xn"][:, k, cols],
                                                 start=(k == 0), stop=(k == KD - 1))
                    ctx.tok_mm[bank] = ctx.bump_t(ins)
                    if wi == 0:
                        wl = [("SEM_T", ctx.tok_mm[bank])]
                        if tt == 0:
                            wl.append(("SEM_V", xs_free[j % 2]))
                        SS(ctx, "activation", xs[:, cols], B["pb"][tt][:],
                           AF.Sigmoid, bias=B["bga_f"][:, col:col + 1], waits=wl,
                           evac=True)
                        ctx.tok_ev[bank] = ("E", ctx.c["E"])
                    elif wi == 1:
                        tb = tvq2[tt % 2]
                        SS(ctx, "activation", tb, B["pb"][tt][:],
                           AF.Tanh, bias=B["bva_f"][:, col:col + 1],
                           waits=[("SEM_T", ctx.tok_mm[bank]),
                                  ("SEM_V", tvq_tok[tt % 2])], evac=True)
                        ctx.tok_ev[bank] = ("E", ctx.c["E"])
                        tvq_tok[tt % 2] = V(ctx, "tensor_mul", xs[:, cols],
                                            xs[:, cols], tb,
                                            waits=[("SEM_E", ctx.c["E"])])
                    else:
                        sb = sgq2[tt % 2]
                        SS(ctx, "activation", sb, B["pb"][tt][:],
                           AF.Sigmoid, bias=B["bda_f"][:, col:col + 1],
                           waits=[("SEM_T", ctx.tok_mm[bank]),
                                  ("SEM_V", sgq_tok[tt % 2])], evac=True)
                        ctx.tok_ev[bank] = ("E", ctx.c["E"])
                        sgq_tok[tt % 2] = V(ctx, "tensor_scalar", a_t[:, cols], sb,
                                            0.998, 0.001, ALU.mult, ALU.add,
                                            waits=[("SEM_E", ctx.c["E"])])
                slab_free(ctx, slot)
            if ATTN1_NO_SCAN:
                V(ctx, "tensor_copy", h_t, xs, waits=[("dSP", ctx.d.get("dSP", 0))])
                V(ctx, "tensor_copy", P_t, a_t)
            else:
                V(ctx, "tensor_tensor_scan", h_t, a_t, xs, 0.0, ALU.mult, ALU.add,
                  waits=[("dSP", ctx.d.get("dSP", 0))])
                V(ctx, "tensor_tensor_scan", P_t, a_t, a_t, 1.0, ALU.mult, ALU.bypass)
            xs_free[j % 2] = ctx.c["V"]
            V(ctx, "tensor_copy", B["exch"][:, j:j + 1], P_t[:, TC - 1:TC])
            V(ctx, "tensor_copy", B["exch"][:, 32 + j:32 + j + 1], h_t[:, TC - 1:TC])
            if not ATTN1_NO_SPILL:
                ins1 = ins2 = None
                if ctx.y():
                    ctx.wait("SEM_V", ctx.c["V"])
                    ins1 = ctx.eng.dma_start(out=B["sp_h"][j], in_=h_t)
                    ins2 = ctx.eng.dma_start(out=B["sp_p"][j], in_=P_t)
                ctx.dma("dSP", ins1)
                ctx.dma("dSP", ins2)

    ins = None
    if ctx.g():
        ctx.wait("SEM_V", ctx.c["V"])
        ins = ctx.eng.dma_start(out=B["cc_in"][:, :], in_=B["exch"][:])
    ctx.dma("dCC", ins)
    if not SKIP_CC:
        if ctx.g():
            ctx.wait("dCC", ctx.d["dCC"])
            ctx.eng.collective_compute(
                "AllGather", ALU.bypass, replica_groups=[list(range(NCORE))],
                ins=[B["cc_in"].ap().opt()], outs=[B["cc_out"].ap().opt()]
            ).then_inc(ctx.S["cCC"])
            ctx.wait("cCC", 1)
    else:
        for r in range(NCORE):
            ins = None
            if ctx.g():
                ctx.wait("dCC", ctx.d["dCC"])
                ins = ctx.eng.dma_start(out=B["cc_out"].ap()[128 * r:128 * (r + 1), :],
                                        in_=B["cc_in"][:, :])
            ctx.dma("dCC", ins)
    if not ATTN1_NO_AH:
        for r in range(NCORE):
            ins = None
            if ctx.g():
                ins = ctx.eng.dma_start(out=B["AH"][:, r, :],
                                        in_=B["cc_out"].ap()[128 * r:128 * (r + 1), :])
            ctx.dma("dCC", ins)
    if not ATTN1_NO_COMBINE:
        V(ctx, "memset", B["carry_f"][:], 0.0, waits=[("dCC", ctx.d["dCC"])])
        for r in range(NCORE - 1):
            V(ctx, "tensor_scalar", sgq[:, 0:32], B["AH"][:, r, 0:32],
              B["msel_f"][:, r:r + 1], B["mselinv_f"][:, r:r + 1], ALU.mult, ALU.add)
            V(ctx, "tensor_mul", B["carry_f"][:], B["carry_f"][:], sgq[:, 0:32])
            V(ctx, "tensor_scalar", sgq[:, 32:64], B["AH"][:, r, 32:64],
              B["msel_f"][:, r:r + 1], None, ALU.mult)
            V(ctx, "tensor_add", B["carry_f"][:], B["carry_f"][:], sgq[:, 32:64])

    if ATTN1_NO_PASS2:
        return
    rl_sem = ["dRL0", "dRL1"]
    use_tok = [None, None]
    for m in range(4):
        gen_comb_b(m, ctx.c["V"])
        cb_tok = ctx.c["E"]
        for dt in range(KD):
            j = m * KD + dt
            if j >= ATTN1_MDTS:
                continue
            h_t, P_t = hP[j % 2]
            sem = rl_sem[j % 2]
            ins1 = ins2 = None
            if ctx.y():
                if use_tok[j % 2]:
                    ctx.wait_tok(use_tok[j % 2])
                else:
                    ctx.wait("dSP", ctx.d["dSP"])
                ins1 = ctx.eng.dma_start(out=h_t, in_=B["sp_h"][j])
                ins2 = ctx.eng.dma_start(out=P_t, in_=B["sp_p"][j])
            ctx.dma(sem, ins1)
            ctx.dma(sem, ins2)
            V(ctx, "scalar_tensor_tensor", h_t, P_t, B["carry_f"][:, j:j + 1], h_t,
              ALU.mult, ALU.add,
              waits=[(sem, ctx.d[sem]), ("SEM_E", cb_tok)])
            hx_tok = ctx.c["V"]
            G(ctx, "tensor_mul", P_t, h_t, comb_b,
              waits=[("SEM_V", hx_tok), ("SEM_E", cb_tok)])
            out = B["xr"][:, dt, :]
            G(ctx, "tensor_add", out, out, P_t)
            use_tok[j % 2] = ("G", ctx.c["G"])
    ctx.p2g = ctx.c["G"]


def ph_mask_gates(ctx):
    B = ctx.B
    mask_mM = scr_bf(B, 64, [8, 2048])
    glT = scr_f32(B, 0, [128, 128])
    gl_raw = scr_f32(B, 1, [128, 128])
    mw = scr_f32(B, 2, [128, 128])
    mres = scr_f32(B, 4, [128, 16, 8])

    if not ctx.t():
        ctx.join()
    for ts in range(16):
        ins = None
        if ctx.t():
            if ts == 0:
                ctx.wait_tok(ctx.tok_ev.get("b3", ("V", 0)))
            for k in range(KD):
                ins = ctx.eng.matmul(B["pb"][3][:, 8 * ts:8 * (ts + 1)],
                                     B["xn"][:, k, 128 * ts:128 * (ts + 1)],
                                     B["gfw_b"][:, k, :],
                                     start=(k == 0), stop=(k == KD - 1))
        ctx.tok_mm["b3"] = ctx.bump_t(ins)
    V(ctx, "tensor_copy", gl_raw, B["pb"][3][:, 0:128],
      waits=[("SEM_T", ctx.tok_mm["b3"])], evac=True)
    for ts in range(16):
        V(ctx, "tensor_add", _view(gl_raw, [128, 16, 8])[:, ts, :],
          _view(gl_raw, [128, 16, 8])[:, ts, :], B["bias8_b"][:],
          needE=(ts == 0))
    ctx.tok_ev["b3"] = ("E", ctx.c["E"])
    V(ctx, "tensor_copy", _view(glT, [128, 8, 16]),
      _view(gl_raw, [128, 16, 8]).rearrange("p a b -> p b a"))
    cj = [glT[:, 16 * j:16 * (j + 1)] for j in range(8)]
    wj = [mw[:, 16 * j:16 * (j + 1)] for j in range(8)]
    V(ctx, "tensor_max", wj[0], cj[0], cj[1])
    V(ctx, "tensor_max", wj[1], cj[2], cj[3])
    V(ctx, "tensor_max", wj[2], cj[4], cj[5])
    V(ctx, "tensor_max", wj[3], cj[6], cj[7])
    V(ctx, "tensor_max", wj[4], wj[0], wj[1])
    V(ctx, "tensor_max", wj[5], wj[2], wj[3])
    V(ctx, "tensor_max", wj[6], wj[4], wj[5])
    for j in range(8):
        V(ctx, "tensor_tensor",
          mres[:, :, j:j + 1].rearrange("p a b -> p (a b)"),
          cj[j], wj[6], ALU.is_ge)
    for q in range(4):
        ins = None
        if ctx.t():
            ctx.wait_tok(ctx.tok_ev.get("b2", ("V", 0)))
            ctx.wait("SEM_V", ctx.c["V"])
            for r in range(4):
                ts = 4 * q + r
                ins = ctx.eng.transpose(B["pb"][2][0:8, 128 * r:128 * (r + 1)],
                                        mres[:, ts, :], B["id_f"][:])
        ctx.tok_mm["b2"] = ctx.bump_t(ins)
        SS(ctx, "copy", mask_mM[:, 512 * q:512 * (q + 1)], B["pb"][2][0:8, :],
           waits=[("SEM_T", ctx.tok_mm["b2"])], evac=True)
        ctx.tok_ev["b2"] = ("E", ctx.c["E"])


def ph_ffn1(ctx):
    B = ctx.B
    h1 = scr_bf(B, 0, [128, 32, 1024])
    mask_mM = scr_bf(B, 64, [8, 2048])
    mask_b = scr_bf(B, 68, [128, 2, 512])
    tbuf = [scr_f32(B, min(70, SCR_KB - 4), [128, 512]), scr_f32(B, min(72, SCR_KB - 2), [128, 512])]

    w1slabs = [Slab("dSL0", ctx.c["T"]), Slab("dSL1", ctx.c["T"])]
    w1view = [wst_bf(B, 4 * i, [128, KD, 256]) for i in range(2)]
    w2slabs = [Slab("dSL2", ctx.c["T"]), Slab("dSL3", ctx.c["T"])]
    w2view = [wst_bf(B, 8 + 8 * i, [128, 32, 128]) for i in range(2)]

    def w1src(e, fq):
        return B["w1"][e][:, 256 * fq:256 * (fq + 1)].rearrange("(k p) f -> p k f", p=128)

    def w2src(e, v):
        return B["w2"][e][:, 128 * v:128 * (v + 1)].rearrange("(k p) v -> p k v", p=128)

    w1seq = [(e, fq) for th in range(2) for e in range(8) for fq in range(16)]
    w2seq = [(e, v) for th in range(2) for e in range(8) for v in range(KD)]

    slab_load(ctx, w1slabs[0], w1view[0], w1src(*w1seq[0]))
    n1 = 1
    slab_load(ctx, w2slabs[0], w2view[0], w2src(*w2seq[0]))
    n2 = 1

    y_tok = 0
    for th in range(2):
        for e in range(8):
            for tt in range(2):
                ins = None
                if ctx.t():
                    ctx.wait_tok(ctx.tok_ev.get("b2", ("V", 0)))
                    ins = ctx.eng.matmul(
                        B["pb"][2][:], B["ematb"][0:8, 128 * e:128 * (e + 1)],
                        mask_mM[:, 1024 * th + 512 * tt:1024 * th + 512 * (tt + 1)],
                        start=True, stop=True)
                ctx.tok_mm["b2"] = ctx.bump_t(ins)
                wl = [("SEM_T", ctx.tok_mm["b2"])]
                if tt == 0:
                    wl.append(("SEM_E", ctx.c["E"]))
                SS(ctx, "copy", mask_b[:, tt, :], B["pb"][2][:], waits=wl, evac=True)
                ctx.tok_ev["b2"] = ("E", ctx.c["E"])
            for fq in range(16):
                j1 = th * 128 + e * 16 + fq
                slot = w1slabs[j1 % 2]
                if n1 < len(w1seq):
                    slab_load(ctx, w1slabs[n1 % 2], w1view[n1 % 2], w1src(*w1seq[n1]))
                    n1 += 1
                slab_wait(ctx, slot)
                for f2 in range(2):
                    ft = fq * 2 + f2
                    for tt in range(2):
                        bank = "b%d" % (2 * f2 + tt)
                        ins = None
                        if ctx.t():
                            ctx.wait_tok(ctx.tok_ev.get(bank, ("V", 0)))
                            for k in range(KD):
                                ins = ctx.eng.matmul(
                                    B["pb"][2 * f2 + tt][:],
                                    w1view[j1 % 2][:, k, 128 * f2:128 * (f2 + 1)],
                                    B["xn"][:, k, 1024 * th + 512 * tt:
                                            1024 * th + 512 * (tt + 1)],
                                    start=(k == 0), stop=(k == KD - 1))
                        ctx.tok_mm[bank] = ctx.bump_t(ins)
                        bcol = e * 32 + ft
                        first = (fq == 0 and f2 == 0 and tt == 0)
                        if ft % 2 == 0:
                            wl = [("SEM_T", max(ctx.tok_mm[bank], y_tok if first else 0))]
                            SS(ctx, "activation", h1[:, ft, 512 * tt:512 * (tt + 1)],
                               B["pb"][2 * f2 + tt][:], AF.Relu,
                               bias=B["b1c_f"][:, bcol:bcol + 1], waits=wl, evac=True)
                            ctx.tok_ev[bank] = ("E", ctx.c["E"])
                        else:
                            wl = [("SEM_T", max(ctx.tok_mm[bank], y_tok if fq == 0 else 0))]
                            V(ctx, "tensor_scalar", h1[:, ft, 512 * tt:512 * (tt + 1)],
                              B["pb"][2 * f2 + tt][:], B["b1c_f"][:, bcol:bcol + 1], 0.0,
                              ALU.add, ALU.max, waits=wl, evac=True)
                            ctx.tok_ev[bank] = ("E", ctx.c["E"])
                slab_free(ctx, slot)
            h1_tok_e = ctx.c["E"]
            for v in range(KD):
                j2 = th * 64 + e * 8 + v
                slot = w2slabs[j2 % 2]
                if n2 < len(w2seq):
                    slab_load(ctx, w2slabs[n2 % 2], w2view[n2 % 2], w2src(*w2seq[n2]))
                    n2 += 1
                slab_wait(ctx, slot)
                for tt in range(2):
                    bank = "b%d" % (2 + tt)
                    pbank = B["pb"][2 + tt]
                    ins = None
                    if ctx.t():
                        ctx.wait_tok(ctx.tok_ev.get(bank, ("V", 0)))
                        if v == 0 and tt == 0:
                            ctx.wait("SEM_E", h1_tok_e)
                        for fc in range(32):
                            ins = ctx.eng.matmul(pbank[:],
                                                 w2view[j2 % 2][:, fc, :],
                                                 h1[:, fc, 512 * tt:512 * (tt + 1)],
                                                 start=(fc == 0), stop=(fc == 31))
                    ctx.tok_mm[bank] = ctx.bump_t(ins)
                    bcol = e * 8 + v
                    V(ctx, "scalar_tensor_tensor", tbuf[tt], pbank[:],
                      B["b2c_f"][:, bcol:bcol + 1], mask_b[:, tt, :],
                      ALU.add, ALU.mult,
                      waits=[("SEM_T", ctx.tok_mm[bank]), ("SEM_E", ctx.c["E"])])
                    out = B["xr"][:, v, 1024 * th + 512 * tt:
                                  1024 * th + 512 * (tt + 1)]
                    V(ctx, "tensor_add", out, out, tbuf[tt], evac=True)
                    ctx.tok_ev[bank] = ("E", ctx.c["E"])
                slab_free(ctx, slot)
            y_tok = ctx.c["T"]


def ph_store(ctx):
    B = ctx.B
    for i in range(KD):
        ins = None
        if ctx.y():
            if i == 0:
                ctx.wait("SEM_V", ctx.c["V"])
                ctx.wait("SEM_S", ctx.c["S"])
                ctx.wait("SEM_E", ctx.c["E"])
                ctx.wait("SEM_G", ctx.c["G"])
                ctx.wait("SEM_T", ctx.c["T"])
            ins = ctx.eng.dma_start(out=B["y"][128 * i:128 * (i + 1), :],
                                    in_=B["xr"][:, i, :])
        ctx.dma("dST", ins)
    if ctx.y():
        ctx.wait("dST", ctx.d["dST"])


def emit_program(ctx, layers):
    ph_preload(ctx, layers)
    if 0 in layers:
        if "ln0" not in SKIP_PHASES:
            ph_ln(ctx, add_resid=True)
        if "attn0" not in SKIP_PHASES:
            ph_attn0(ctx)
        if "ffn0" not in SKIP_PHASES:
            ph_ffn0(ctx)
    if 1 in layers:
        if "ln1" not in SKIP_PHASES:
            ph_ln(ctx, add_resid=False)
        if "comb" not in SKIP_PHASES:
            ph_comb_gates(ctx)
        if "attn1" not in SKIP_PHASES:
            ph_attn1(ctx)
        if "mask" not in SKIP_PHASES:
            ph_mask_gates(ctx)
        if "ffn1" not in SKIP_PHASES:
            ph_ffn1(ctx)
    ph_store(ctx)



def host_consts(layers):
    c = {}
    c["cones"] = np.ones((128, 128), np.float32)
    c["cident"] = np.eye(128, dtype=np.float32)
    em = np.zeros((16, 2048), np.float32)
    for k in range(16):
        em[k, 128 * k:128 * (k + 1)] = 1.0
    c["cematb"] = em
    if 0 in layers:
        t = np.arange(128)
        tp = t[:, None]
        tq = t[None, :]
        same = (tq // 32) == (tp // 32)
        c["cdiff"] = np.where(same, tq % 32 - tp % 32, 0).astype(np.float32)
        c["clt"] = (same & ((tq % 32) >= (tp % 32))).astype(np.float32)
    if 1 in layers:
        e4 = np.zeros((4, 512), np.float32)
        for k in range(4):
            e4[k, 128 * k:128 * (k + 1)] = 1.0
        c["ce4"] = e4
    return c


def kernel(**inp):
    layers_list = LAYERS_LIST or ([(0, 1)] if FUSED else [(0,), (1,)])
    x = np.asarray(inp["L0_attn_x"])[0]
    x_sh = [np.ascontiguousarray(x[TC * cc:TC * (cc + 1), :].T) for cc in range(NCORE)]

    base0 = {
        "wq": np.ascontiguousarray(np.asarray(inp["L0_attn_W_q"]).T),
        "wk": np.ascontiguousarray(np.asarray(inp["L0_attn_W_k"]).T),
        "wv": np.ascontiguousarray(np.asarray(inp["L0_attn_W_v"]).T),
        "wg": np.ascontiguousarray(np.asarray(inp["L0_attn_W_g"]).T),
        "wo": np.ascontiguousarray(np.asarray(inp["L0_attn_W_o"]).T),
        "ld0": np.asarray(inp["L0_attn_log_decay"])[0:1].reshape(1, 1).astype(np.float32),
        "esel": np.ascontiguousarray(np.asarray(inp["L0_ffn_expert_sel"]).T),
        "keys": np.ascontiguousarray(np.asarray(inp["L0_ffn_keys"])),
        "values": np.ascontiguousarray(np.asarray(inp["L0_ffn_values"])),
    }
    bg = np.asarray(inp["L1_attn_bg"])
    bv = np.asarray(inp["L1_attn_bv"])
    bd = np.asarray(inp["L1_attn_bd"])
    b1 = np.asarray(inp["L1_ffn_b1"])
    b2 = np.asarray(inp["L1_ffn_b2"])
    base1 = {
        "wga": np.ascontiguousarray(np.asarray(inp["L1_attn_Wg"])),
        "wva": np.ascontiguousarray(np.asarray(inp["L1_attn_Wv"])),
        "wda": np.ascontiguousarray(np.asarray(inp["L1_attn_Wd"])),
        "bga": np.ascontiguousarray(bg.reshape(4, 8, 128).transpose(2, 0, 1).reshape(128, 32)),
        "bva": np.ascontiguousarray(bv.reshape(4, 8, 128).transpose(2, 0, 1).reshape(128, 32)),
        "bda": np.ascontiguousarray(bd.reshape(4, 8, 128).transpose(2, 0, 1).reshape(128, 32)),
        "gw": np.ascontiguousarray(np.asarray(inp["L1_attn_gate_W"])),
        "gfw": np.ascontiguousarray(np.asarray(inp["L1_ffn_gate_w"]).T),
        "gfb": np.asarray(inp["L1_ffn_gate_b"]).reshape(1, 8).astype(np.float32),
        "efb": np.asarray(inp["L1_ffn_expert_bias"]).reshape(1, 8).astype(np.float32),
        "w1": np.ascontiguousarray(np.asarray(inp["L1_ffn_w1"])),
        "b1c": np.ascontiguousarray(b1.reshape(8, 32, 128).transpose(2, 0, 1).reshape(128, 256)),
        "w2": np.ascontiguousarray(np.asarray(inp["L1_ffn_w2"])),
        "b2c": np.ascontiguousarray(b2.reshape(8, 8, 128).transpose(2, 0, 1).reshape(128, 64)),
    }

    cur = x_sh
    for layers in layers_list:
        consts = host_consts(layers)
        nc = build_program(layers)
        in_maps = []
        for cc in range(NCORE):
            m = dict(consts)
            m["x"] = np.ascontiguousarray(cur[cc].astype(np.float32))
            if 0 in layers:
                m.update(base0)
            if 1 in layers:
                m.update(base1)
                msel = np.zeros((128, 8), np.float32)
                msel[:, :cc] = 1.0
                m["cmsel"] = msel
                m["cmselinv"] = np.ascontiguousarray(1.0 - msel)
            in_maps.append(m)
        res = run_bass_kernel_spmd(nc, in_maps, core_ids=list(range(NCORE)))
        cur = [res.results[cc]["y"] for cc in range(NCORE)]

    out = np.concatenate([s.T for s in cur], axis=0)[None]
    return np.ascontiguousarray(out.astype(np.float32))


# version 16 (final)
# speedup vs baseline: 1.0364x; 1.0364x over previous
import contextlib
import numpy as np
import concourse.bass as bass
import concourse.mybir as mybir
import concourse.bass_utils as _bass_utils
from concourse.bass_utils import run_bass_kernel_spmd

LDW_OPT = True
_orig_bvo = _bass_utils.bir_verify_and_optimise


def _bvo_ldwopt(*a, **kw):
    orig_run = _bass_utils.run_command

    def run2(cmd, **k):
        if LDW_OPT:
            cmd = ["--enable-ldw-opt=true" if c == "--enable-ldw-opt=false" else c
                   for c in cmd]
        return orig_run(cmd, **k)

    _bass_utils.run_command = run2
    try:
        return _orig_bvo(*a, **kw)
    finally:
        _bass_utils.run_command = orig_run


_bass_utils.bir_verify_and_optimise = _bvo_ldwopt

F32 = mybir.dt.float32
BF16 = mybir.dt.bfloat16
AF = mybir.ActivationFunctionType
ALU = mybir.AluOpType
AX = mybir.AxisListType

NCORE = 8
D = 1024
T_FULL = 16384
TC = T_FULL // NCORE
KD = 8
EPS = 1e-5
FUSED = True
LAYERS_LIST = None
SKIP_CC = False
SCR_KB = 74
SKIP_PHASES = set()
ATTN1_MDTS = 32
ATTN1_NO_SCAN = False
ATTN1_NO_PROJ = False
ATTN1_NO_COMBINE = False
ATTN1_NO_AH = False
ATTN1_NO_SPILL = False
ATTN1_NO_PASS2 = False


class Ctx:
    def __init__(self, eng_name, eng, S, B):
        self.e = eng_name
        self.eng = eng
        self.S = S
        self.B = B
        self.c = {"T": 0, "V": 0, "S": 0, "E": 0}
        self.d = {}
        self.tok_mm = {}
        self.tok_ev = {}
        self.waited = {}

    def t(self):
        return self.e == "tensor"

    def v(self):
        return self.e == "vector"

    def s(self):
        return self.e == "scalar"

    def g(self):
        return self.e == "gpsimd"

    def y(self):
        return self.e == "sync"

    def bump_t(self, ins):
        self.c["T"] += 1
        if ins is not None:
            ins.then_inc(self.S["SEM_T"], 1)
        return self.c["T"]

    def dma(self, sem, ins=None):
        self.d[sem] = self.d.get(sem, 0) + 16
        if ins is not None:
            ins.then_inc(self.S[sem], 16)
        return self.d[sem]

    def wait(self, sem, val):
        if val > 0 and self.waited.get(sem, 0) < val:
            self.eng.wait_ge(self.S[sem], val)
            self.waited[sem] = val

    def wait_tok(self, tok):
        key, val = tok
        self.wait("SEM_" + key, val)

    def join(self):
        own = {"tensor": "T", "vector": "V", "scalar": "S"}.get(self.e)
        for k in ("T", "V", "S", "E"):
            if k != own:
                self.wait("SEM_" + k, self.c[k])

    def prewait(self, banks):
        if self.t():
            best = {}
            for b in banks:
                tok = self.tok_ev.get(b)
                if tok:
                    key, val = tok
                    best[key] = max(best.get(key, 0), val)
            for key, val in best.items():
                self.wait("SEM_" + key, val)


def V(ctx, name, *args, waits=(), evac=False, needE=False, **kw):
    ins = None
    if ctx.v():
        for s, v in waits:
            ctx.wait(s, v)
        ctx.wait("SEM_V", ctx.c["V"])
        ctx.wait("SEM_E", ctx.c["E"])
        ins = getattr(ctx.eng, name)(*args, **kw)
    key = "E" if evac else "V"
    ctx.c[key] += 1
    if ins is not None:
        ins.then_inc(ctx.S["SEM_" + key], 1)
    return ctx.c[key]


def SS(ctx, name, *args, waits=(), evac=False, needE=False, **kw):
    ins = None
    if ctx.s():
        for s, v in waits:
            ctx.wait(s, v)
        ctx.wait("SEM_S", ctx.c["S"])
        ctx.wait("SEM_E", ctx.c["E"])
        ins = getattr(ctx.eng, name)(*args, **kw)
    key = "E" if evac else "S"
    ctx.c[key] += 1
    if ins is not None:
        ins.then_inc(ctx.S["SEM_" + key], 1)
    return ctx.c[key]


class Slab:
    def __init__(self, sem, free_tok=None):
        self.sem = sem
        self.loaded = 0
        self.free_tok = free_tok


def slab_load(ctx, slab, dst, src):
    ins = None
    if ctx.g():
        if slab.free_tok:
            ctx.wait("SEM_T", slab.free_tok)
        ins = ctx.eng.dma_start(out=dst, in_=src)
    slab.loaded = ctx.dma(slab.sem, ins)


def slab_wait(ctx, slab):
    if ctx.t():
        ctx.wait(slab.sem, slab.loaded)


def slab_free(ctx, slab):
    slab.free_tok = ctx.c["T"]



def build_program(layers):
    nc = bass.Bass("TRN2", target_bir_lowering=False)
    B = {}

    def di(name, shape):
        B[name] = nc.dram_tensor(name, shape, F32, kind="ExternalInput").ap()

    di("x", [D, TC])
    di("cones", [128, 128])
    di("cident", [128, 128])
    di("cematb", [16, 2048])
    if 0 in layers:
        for w in ("wq", "wk", "wv", "wg", "wo"):
            di(w, [D, D])
        di("ld0", [1, 1])
        di("cdiff", [128, 128])
        di("clt", [128, 128])
        di("esel", [D, 16])
        di("keys", [16, D, 128])
        di("values", [16, 128, D])
    if 1 in layers:
        for w in ("wga", "wva", "wda"):
            di(w, [4, D, D])
        for b in ("bga", "bva", "bda"):
            di(b, [128, 32])
        di("gw", [D, 4])
        di("gfw", [D, 8])
        di("gfb", [1, 8])
        di("efb", [1, 8])
        di("w1", [8, D, 4096])
        di("b1c", [128, 256])
        di("w2", [8, 4096, D])
        di("b2c", [128, 64])
        di("ce4", [4, 512])
        di("cmsel", [128, 8])
        di("cmselinv", [128, 8])
        B["sp_h"] = nc.dram_tensor("sp_h", [32, 128, TC], F32).ap()
        B["sp_p"] = nc.dram_tensor("sp_p", [32, 128, TC], F32).ap()
        B["cc_in"] = nc.dram_tensor("cc_in", [128, 64], F32)
        B["cc_out"] = nc.dram_tensor("cc_out", [NCORE * 128, 64], F32)
    B["y"] = nc.dram_tensor("y", [D, TC], F32, kind="ExternalOutput").ap()

    with contextlib.ExitStack() as es:
        e = es.enter_context
        xr = e(nc.sbuf_tensor("xr", [128, KD, TC], F32))
        xn = e(nc.sbuf_tensor("xn", [128, KD, TC], BF16))
        WST = e(nc.sbuf_tensor("WST", [128, 12288], BF16))
        SCR = e(nc.sbuf_tensor("SCR", [128, SCR_KB * 512], BF16))
        ones_f = e(nc.sbuf_tensor("ones_f", [128, 128], F32))
        id_f = e(nc.sbuf_tensor("id_f", [128, 128], F32))
        ematb = e(nc.sbuf_tensor("ematb", [16, 2048], BF16))
        B.update(xr=xr, xn=xn, WST=WST, SCR=SCR, SCRF=SCR.bitcast(F32),
                 ones_f=ones_f, id_f=id_f, ematb=ematb)
        if 0 in layers:
            B["diff_f"] = e(nc.sbuf_tensor("diff_f", [128, 128], F32))
            B["lt_f"] = e(nc.sbuf_tensor("lt_f", [128, 128], F32))
            B["maskT"] = e(nc.sbuf_tensor("maskT", [128, 128], F32))
            B["esel_b"] = e(nc.sbuf_tensor("esel_b", [128, KD, 16], BF16))
            B["ld0_f"] = e(nc.sbuf_tensor("ld0_f", [1, 1], F32))
            B["lnd_col"] = e(nc.sbuf_tensor("lnd_col", [128, 1], F32))
        if 1 in layers:
            B["gw_b"] = e(nc.sbuf_tensor("gw_b", [128, KD, 4], BF16))
            B["gfw_b"] = e(nc.sbuf_tensor("gfw_b", [128, KD, 8], BF16))
            for b in ("bga", "bva", "bda"):
                B[b + "_f"] = e(nc.sbuf_tensor(b + "_f", [128, 32], F32))
            B["b1c_f"] = e(nc.sbuf_tensor("b1c_f", [128, 256], F32))
            B["b2c_f"] = e(nc.sbuf_tensor("b2c_f", [128, 64], F32))
            B["e4f"] = e(nc.sbuf_tensor("e4f", [4, 512], F32))
            B["msel_f"] = e(nc.sbuf_tensor("msel_f", [128, 8], F32))
            B["mselinv_f"] = e(nc.sbuf_tensor("mselinv_f", [128, 8], F32))
            B["gfb_f"] = e(nc.sbuf_tensor("gfb_f", [1, 8], F32))
            B["efb_f"] = e(nc.sbuf_tensor("efb_f", [1, 8], F32))
            B["b8_f"] = e(nc.sbuf_tensor("b8_f", [1, 8], F32))
            B["bias8_b"] = e(nc.sbuf_tensor("bias8_b", [128, 8], F32))
            B["exch"] = e(nc.sbuf_tensor("exch", [128, 64], F32))
            B["AH"] = e(nc.sbuf_tensor("AH", [128, NCORE, 64], F32))
            B["carry_f"] = e(nc.sbuf_tensor("carry_f", [128, 32], F32))
        pb = [e(nc.psum_tensor(f"pb{i}", [128, 512], F32)) for i in range(4)]
        pO = [e(nc.psum_tensor(f"pO{i}", [128, 1024], F32)) for i in range(2)]
        B["pb"] = pb
        B["pO"] = pO

        sem_names = (["SEM_T", "SEM_V", "SEM_S", "SEM_E", "dXL", "dCL", "dVL", "dSP",
                      "dRL0", "dRL1", "dCC", "cCC", "dST"]
                     + [f"dSL{i}" for i in range(6)])
        S = {n: e(nc.semaphore(n)) for n in sem_names}
        block = e(nc.Block())

        def run(eng_name):
            def f(eng):
                ctx = Ctx(eng_name, eng, S, B)
                emit_program(ctx, layers)
            return f

        block.tensor(run("tensor"))
        block.vector(run("vector"))
        block.scalar(run("scalar"))
        block.gpsimd(run("gpsimd"))
        block.sync(run("sync"))

    return nc



def _view(ap, shape):
    if len(shape) > 2:
        names = " ".join(f"d{i}" for i in range(1, len(shape)))
        ap = ap.rearrange(f"p ({names}) -> p " + names,
                          **{f"d{i}": shape[i] for i in range(1, len(shape) - 1)})
    return ap


def scr_bf(B, kb, shape):
    off = int(kb * 512)
    n = int(np.prod(shape[1:]))
    assert off + n <= SCR_KB * 512, (kb, shape)
    return _view(B["SCR"][0:shape[0], off:off + n], shape)


def scr_f32(B, kb, shape):
    off = int(kb * 256)
    n = int(np.prod(shape[1:]))
    assert off + n <= SCR_KB * 256, (kb, shape)
    return _view(B["SCRF"][0:shape[0], off:off + n], shape)


def wst_bf(B, kb, shape):
    off = int(kb * 512)
    n = int(np.prod(shape[1:]))
    assert off + n <= 12288, (kb, shape)
    return _view(B["WST"][0:shape[0], off:off + n], shape)



def ph_preload(ctx, layers):
    B = ctx.B
    for i in range(KD):
        ins = None
        if ctx.y():
            ins = ctx.eng.dma_start(out=B["xr"][:, i, :],
                                    in_=B["x"][128 * i:128 * (i + 1), :])
        ctx.dma("dXL", ins)
    consts = [("cones", "ones_f"), ("cident", "id_f")]
    if 0 in layers:
        consts += [("cdiff", "diff_f"), ("clt", "lt_f"), ("ld0", "ld0_f")]
    if 1 in layers:
        consts += [("gfb", "gfb_f"), ("efb", "efb_f"), ("ce4", "e4f"),
                   ("cmsel", "msel_f"), ("cmselinv", "mselinv_f"),
                   ("bga", "bga_f"), ("bva", "bva_f"), ("bda", "bda_f"),
                   ("b1c", "b1c_f"), ("b2c", "b2c_f")]
    for src, dst in consts:
        ins = None
        if ctx.g():
            ins = ctx.eng.dma_start(out=B[dst][:], in_=B[src])
        ctx.dma("dCL", ins)
    casts = [(B["ematb"][:], "cematb", None)]
    if 0 in layers:
        casts.append((B["esel_b"][:], "esel",
                      B["esel"].rearrange("(k p) e -> p k e", p=128)))
    if 1 in layers:
        casts.append((B["gw_b"][:], "gw",
                      B["gw"].rearrange("(k p) e -> p k e", p=128)))
        casts.append((B["gfw_b"][:], "gfw",
                      B["gfw"].rearrange("(k p) e -> p k e", p=128)))
    for dst, src_name, src_ap in casts:
        ins = None
        if ctx.g():
            ins = ctx.eng.dma_start(out=dst, in_=B[src_name] if src_ap is None else src_ap)
        ctx.dma("dCL", ins)

    dcl = ctx.d["dCL"]
    if 0 in layers:
        SS(ctx, "activation", B["ld0_f"][:], B["ld0_f"][:], AF.Sigmoid,
           waits=[("dCL", dcl)])
        SS(ctx, "activation", B["ld0_f"][:], B["ld0_f"][:], AF.Ln)
        ins = None
        if ctx.t():
            ctx.wait("SEM_S", ctx.c["S"])
            ctx.wait("dCL", dcl)
            ins = ctx.eng.matmul(B["pb"][0][:, 0:1], B["ones_f"][0:1, :],
                                 B["ld0_f"][:], start=True, stop=True)
        ctx.tok_mm["b0"] = ctx.bump_t(ins)
        V(ctx, "tensor_copy", B["lnd_col"][:], B["pb"][0][:, 0:1],
          waits=[("SEM_T", ctx.tok_mm["b0"]), ("dCL", dcl)], evac=True)
        ctx.tok_ev["b0"] = ("E", ctx.c["E"])
        V(ctx, "tensor_scalar_mul", B["maskT"][:], B["diff_f"][:], B["lnd_col"][:])
        SS(ctx, "activation", B["maskT"][:], B["maskT"][:], AF.Exp,
           waits=[("SEM_V", ctx.c["V"])])
        V(ctx, "tensor_mul", B["maskT"][:], B["maskT"][:], B["lt_f"][:],
          waits=[("SEM_S", ctx.c["S"])])

    if 1 in layers:
        V(ctx, "tensor_add", B["b8_f"][:], B["gfb_f"][:], B["efb_f"][:],
          waits=[("dCL", dcl)])
        ins = None
        if ctx.t():
            ctx.wait("SEM_V", ctx.c["V"])
            ctx.wait("dCL", dcl)
            ins = ctx.eng.matmul(B["pb"][1][:, 0:8], B["ones_f"][0:1, :],
                                 B["b8_f"][:], start=True, stop=True)
        ctx.tok_mm["b1"] = ctx.bump_t(ins)
        SS(ctx, "copy", B["bias8_b"][:], B["pb"][1][:, 0:8],
           waits=[("SEM_T", ctx.tok_mm["b1"])], evac=True)
        ctx.tok_ev["b1"] = ("E", ctx.c["E"])

    if ctx.e in ("tensor", "vector", "scalar"):
        ctx.wait("dXL", ctx.d["dXL"])
        ctx.wait("dCL", ctx.d["dCL"])


def ph_ln(ctx, add_resid):
    B = ctx.B
    sq = scr_f32(B, 16, [128, KD, 512])
    mean = scr_f32(B, 32, [128, 512])
    m2 = scr_f32(B, 34, [128, 512])
    var = scr_f32(B, 36, [128, 512])
    lnv = scr_f32(B, 38, [128, 512])
    rstd = scr_f32(B, 40, [128, 512])
    bterm = scr_f32(B, 42, [128, 512])
    xtmp = scr_f32(B, 44, [128, 512])
    xnf = [scr_f32(B, 46, [128, 512]), scr_f32(B, 48, [128, 512])]
    cast_tok = [0, 0]
    ctx.ln_tok = [0, 0, 0, 0]

    ctx.join()
    sq_tok = 0
    for tt in range(4):
        cols = slice(512 * tt, 512 * (tt + 1))
        for i in range(KD):
            SS(ctx, "activation", sq[:, i, :], B["xr"][:, i, cols], AF.Square,
               waits=([("SEM_T", sq_tok)] if i == 0 else []))
        ins = None
        if ctx.t():
            ctx.wait_tok(ctx.tok_ev.get("b%d" % tt, ("V", 0)))
            for i in range(KD):
                ins = ctx.eng.matmul(B["pb"][tt][:], B["ones_f"][:],
                                     B["xr"][:, i, cols],
                                     start=(i == 0), stop=(i == KD - 1))
        ctx.tok_mm["b%d" % tt] = ctx.bump_t(ins)
        okey = "o%d%d" % (tt // 2, tt % 2)
        pseg = B["pO"][tt // 2][:, 512 * (tt % 2):512 * (tt % 2 + 1)]
        ins = None
        if ctx.t():
            ctx.wait_tok(ctx.tok_ev.get(okey, ("V", 0)))
            ctx.wait("SEM_S", ctx.c["S"])
            for i in range(KD):
                ins = ctx.eng.matmul(pseg, B["ones_f"][:], sq[:, i, :],
                                     start=(i == 0), stop=(i == KD - 1))
        ctx.tok_mm[okey] = ctx.bump_t(ins)
        sq_tok = ctx.c["T"]
    for tt in range(4):
        cols = slice(512 * tt, 512 * (tt + 1))
        okey = "o%d%d" % (tt // 2, tt % 2)
        pseg = B["pO"][tt // 2][:, 512 * (tt % 2):512 * (tt % 2 + 1)]
        V(ctx, "tensor_scalar_mul", mean, B["pb"][tt][:], 1.0 / D,
          waits=[("SEM_T", ctx.tok_mm[okey])], evac=True)
        ctx.tok_ev["b%d" % tt] = ("E", ctx.c["E"])
        V(ctx, "tensor_mul", m2, mean, mean, needE=True)
        V(ctx, "scalar_tensor_tensor", var, pseg, 1.0 / D, m2,
          ALU.mult, ALU.subtract, evac=True)
        ctx.tok_ev[okey] = ("E", ctx.c["E"])
        V(ctx, "tensor_scalar_add", var, var, EPS, needE=True)
        SS(ctx, "activation", lnv, var, AF.Ln,
           waits=[("SEM_V", ctx.c["V"]), ("SEM_E", ctx.c["E"])])
        SS(ctx, "activation", rstd, lnv, AF.Exp, scale=-0.5)
        V(ctx, "scalar_tensor_tensor", bterm, mean, -1.0, rstd, ALU.mult, ALU.mult,
          waits=[("SEM_S", ctx.c["S"])], needE=True)
        for i in range(KD):
            slot = i % 2
            V(ctx, "tensor_mul", xtmp, B["xr"][:, i, cols], rstd,
              waits=[("SEM_S", cast_tok[slot])])
            V(ctx, "tensor_add", xnf[slot], xtmp, bterm)
            if add_resid:
                V(ctx, "tensor_add", B["xr"][:, i, cols], B["xr"][:, i, cols],
                  xnf[slot])
            SS(ctx, "copy", B["xn"][:, i, cols], xnf[slot],
               waits=[("SEM_V", ctx.c["V"])])
            cast_tok[slot] = ctx.c["S"]
        ctx.ln_tok[tt] = ctx.c["S"]


def ph_attn0(ctx):
    B = ctx.B
    Qb = scr_bf(B, 0, [128, KD, 1024])
    Kb = scr_bf(B, 16, [128, KD, 1024])
    Vt = scr_bf(B, 32, [128, KD, 1024])
    Gb = scr_bf(B, 48, [128, KD, 1024])
    gO = scr_bf(B, 64, [128, KD, 512])
    Sb = [scr_bf(B, 72, [128, 128]), scr_bf(B, 72.5, [128, 128])]

    slabs = [Slab(f"dSL{i}", ctx.c["T"]) for i in range(3)]
    slab_view = [wst_bf(B, 8 * i, [128, KD, 512]) for i in range(3)]

    def wsrc(name, h):
        return B[name][:, 512 * h:512 * (h + 1)].rearrange("(k p) f -> p k f", p=128)

    seq = []
    for tg in range(2):
        for w in ("wq", "wk", "wv", "wg", "wo"):
            for h in range(2):
                seq.append((w, h))

    for j in range(2):
        slab_load(ctx, slabs[j % 3], slab_view[j % 3], wsrc(*seq[j]))
    nload = 2

    if not ctx.t():
        ctx.join()

    gO_wo_tok = 0
    for tg in range(2):
        for wi, w in enumerate(("wq", "wk", "wv", "wg")):
            for h in range(2):
                j = tg * 10 + wi * 2 + h
                slot = slabs[j % 3]
                sview = slab_view[j % 3]
                if nload < len(seq):
                    slab_load(ctx, slabs[nload % 3], slab_view[nload % 3], wsrc(*seq[nload]))
                    nload += 1
                slab_wait(ctx, slot)
                if w in ("wq", "wk", "wg"):
                    dst = {"wq": Qb, "wk": Kb, "wg": Gb}[w]
                    for i2 in range(4):
                        for tt2 in range(2):
                            bk = (i2 * 2 + tt2) % 2
                            bank = "b%d" % bk
                            pbk = B["pb"][bk]
                            ins = None
                            if ctx.t():
                                ctx.wait("SEM_S", ctx.ln_tok[2 * tg + tt2])
                                ctx.wait_tok(ctx.tok_ev.get(bank, ("V", 0)))
                                for k in range(KD):
                                    ins = ctx.eng.matmul(
                                        pbk[:],
                                        sview[:, k, 128 * i2:128 * (i2 + 1)],
                                        B["xn"][:, k, 1024 * tg + 512 * tt2:
                                                1024 * tg + 512 * (tt2 + 1)],
                                        start=(k == 0), stop=(k == KD - 1))
                            ctx.tok_mm[bank] = ctx.bump_t(ins)
                            out = dst[:, 4 * h + i2, 512 * tt2:512 * (tt2 + 1)]
                            if w == "wg":
                                SS(ctx, "activation", out, pbk[:], AF.Sigmoid,
                                   waits=[("SEM_T", ctx.tok_mm[bank])], evac=True)
                            else:
                                SS(ctx, "copy", out, pbk[:],
                                   waits=[("SEM_T", ctx.tok_mm[bank])], evac=True)
                            ctx.tok_ev[bank] = ("E", ctx.c["E"])
                else:
                    for ts in range(KD):
                        bank = "b%d" % (ts % 2)
                        pbk = B["pb"][ts % 2]
                        ins = None
                        if ctx.t():
                            ctx.wait("SEM_S", ctx.ln_tok[(1024 * tg + 128 * ts) // 512])
                            ctx.wait_tok(ctx.tok_ev.get(bank, ("V", 0)))
                            for k in range(KD):
                                ins = ctx.eng.matmul(
                                    pbk[:],
                                    B["xn"][:, k, 1024 * tg + 128 * ts:
                                            1024 * tg + 128 * (ts + 1)],
                                    sview[:, k, :],
                                    start=(k == 0), stop=(k == KD - 1))
                        ctx.tok_mm[bank] = ctx.bump_t(ins)
                        SS(ctx, "copy", Vt[:, ts, 512 * h:512 * (h + 1)], pbk[:],
                           waits=[("SEM_T", ctx.tok_mm[bank])], evac=True)
                        ctx.tok_ev[bank] = ("E", ctx.c["E"])
                slab_free(ctx, slot)

        for grp in range(2):
            for ts4 in range(4):
                ts = grp * 4 + ts4
                sbank = "b%d" % (2 + ts % 2)
                pbs = B["pb"][2 + ts % 2]
                ins = None
                if ctx.t():
                    ctx.wait_tok(ctx.tok_ev.get(sbank, ("V", 0)))
                    if ts == 0:
                        ctx.wait("SEM_E", ctx.c["E"])
                        ctx.wait("SEM_S", ctx.c["S"])
                    for k in range(KD):
                        ins = ctx.eng.matmul(pbs[:, 0:128],
                                             Kb[:, k, 128 * ts:128 * (ts + 1)],
                                             Qb[:, k, 128 * ts:128 * (ts + 1)],
                                             start=(k == 0), stop=(k == KD - 1))
                ctx.tok_mm[sbank] = ctx.bump_t(ins)
                V(ctx, "tensor_mul", Sb[ts % 2], pbs[:, 0:128], B["maskT"][:],
                  waits=[("SEM_T", ctx.tok_mm[sbank])], evac=True)
                ctx.tok_ev[sbank] = ("E", ctx.c["E"])
                obank = "o%d" % (ts % 2)
                po = B["pO"][ts % 2]
                ins = None
                if ctx.t():
                    ctx.wait_tok(ctx.tok_ev.get(obank, ("V", 0)))
                    ctx.wait("SEM_E", ctx.c["E"])
                    for i in range(KD):
                        ins = ctx.eng.matmul(po[:, 128 * i:128 * (i + 1)],
                                             Vt[:, ts, 128 * i:128 * (i + 1)],
                                             Sb[ts % 2], start=True, stop=True)
                ctx.tok_mm[obank] = ctx.bump_t(ins)
                V(ctx, "tensor_mul",
                  gO[:, :, 128 * ts4:128 * (ts4 + 1)],
                  po[:].rearrange("p (i t) -> p i t", i=KD),
                  Gb[:, :, 128 * ts:128 * (ts + 1)],
                  waits=[("SEM_T", max(ctx.tok_mm[obank], gO_wo_tok))], evac=True)
                ctx.tok_ev[obank] = ("E", ctx.c["E"])
            gO_v_tok = ctx.c["E"]
            for h in range(2):
                j = tg * 10 + 8 + h
                slot = slabs[j % 3]
                sview = slab_view[j % 3]
                if nload < len(seq) and grp == 1:
                    slab_load(ctx, slabs[nload % 3], slab_view[nload % 3], wsrc(*seq[nload]))
                    nload += 1
                slab_wait(ctx, slot)
                for i2 in range(4):
                    bank = "b%d" % (i2 % 2)
                    ins = None
                    if ctx.t():
                        ctx.wait_tok(ctx.tok_ev.get(bank, ("V", 0)))
                        if i2 == 0 and h == 0:
                            ctx.wait("SEM_E", gO_v_tok)
                        for k in range(KD):
                            ins = ctx.eng.matmul(B["pb"][i2 % 2][:],
                                                 sview[:, k, 128 * i2:128 * (i2 + 1)],
                                                 gO[:, k, :],
                                                 start=(k == 0), stop=(k == KD - 1))
                    ctx.tok_mm[bank] = ctx.bump_t(ins)
                    out = B["xr"][:, 4 * h + i2,
                                  1024 * tg + 512 * grp:1024 * tg + 512 * (grp + 1)]
                    V(ctx, "tensor_add", out, out, B["pb"][i2 % 2][:],
                      waits=[("SEM_T", ctx.tok_mm[bank])], evac=True)
                    ctx.tok_ev[bank] = ("E", ctx.c["E"])
                if grp == 1:
                    slab_free(ctx, slot)
            gO_wo_tok = ctx.c["T"]


def ph_ffn0(ctx):
    B = ctx.B
    values_sb = scr_bf(B, 0, [128, 16, 1024])
    scores = scr_bf(B, 32, [128, 16, 512])
    sel = scr_bf(B, 48, [16, 2048])
    sctmp = [scr_bf(B, 52, [128, 512]), scr_bf(B, 53, [128, 512])]

    ctx.join()
    ins = None
    if ctx.g():
        ctx.wait("SEM_T", ctx.c["T"])
        ctx.wait("SEM_S", ctx.c["S"])
        ctx.wait("SEM_E", ctx.c["E"])
        ins = ctx.eng.dma_start(
            out=values_sb, in_=B["values"].rearrange("e p v -> p e v"))
    ctx.dma("dVL", ins)

    for tt in range(4):
        cols = slice(512 * tt, 512 * (tt + 1))
        ins = None
        if ctx.t():
            ctx.wait_tok(ctx.tok_ev.get("b2", ("V", 0)))
            for k in range(KD):
                ins = ctx.eng.matmul(B["pb"][2][0:16, :], B["esel_b"][:, k, :],
                                     B["xn"][:, k, cols],
                                     start=(k == 0), stop=(k == KD - 1))
        ctx.tok_mm["b2"] = ctx.bump_t(ins)
        SS(ctx, "activation", sel[:, cols], B["pb"][2][0:16, :], AF.Sigmoid,
           waits=[("SEM_T", ctx.tok_mm["b2"])], evac=True)
        ctx.tok_ev["b2"] = ("E", ctx.c["E"])

    kslabs = [Slab("dSL0", ctx.c["T"]), Slab("dSL1", ctx.c["T"])]
    kview = [wst_bf(B, 2 * i, [128, KD, 128]) for i in range(2)]

    def ksrc(e):
        return B["keys"][e].rearrange("(k p) h -> p k h", p=128)

    if ctx.e in ("tensor", "vector"):
        ctx.wait("dVL", ctx.d["dVL"])

    slab_load(ctx, kslabs[0], kview[0], ksrc(0))
    pB_tok = 0
    for tt in range(4):
        cols = slice(512 * tt, 512 * (tt + 1))
        for e in range(16):
            j = tt * 16 + e
            if j + 1 < 64:
                slab_load(ctx, kslabs[(j + 1) % 2], kview[(j + 1) % 2], ksrc((j + 1) % 16))
            slot = kslabs[j % 2]
            slab_wait(ctx, slot)
            bank = "b%d" % (e % 2)
            ins = None
            if ctx.t():
                ctx.wait_tok(ctx.tok_ev.get(bank, ("V", 0)))
                for k in range(KD):
                    ins = ctx.eng.matmul(B["pb"][e % 2][:], kview[j % 2][:, k, :],
                                         B["xn"][:, k, cols],
                                         start=(k == 0), stop=(k == KD - 1))
            ctx.tok_mm[bank] = ctx.bump_t(ins)
            slab_free(ctx, slot)
            ins = None
            if ctx.t():
                ctx.wait_tok(ctx.tok_ev.get("b3", ("V", 0)))
                if j == 0:
                    ctx.wait("SEM_E", ctx.c["E"])
                ins = ctx.eng.matmul(B["pb"][3][:], B["ematb"][:, 128 * e:128 * (e + 1)],
                                     sel[:, cols], start=True, stop=True)
            ctx.tok_mm["b3"] = ctx.bump_t(ins)
            SS(ctx, "activation", sctmp[e % 2], B["pb"][e % 2][:], AF.Relu,
               waits=[("SEM_T", ctx.tok_mm[bank]),
                      ("SEM_E", ctx.tok_ev.get("sct%d" % (e % 2), ("E", 0))[1])],
               evac=True)
            ctx.tok_ev[bank] = ("E", ctx.c["E"])
            wl = [("SEM_E", ctx.c["E"]), ("SEM_T", ctx.tok_mm["b3"])]
            if e == 0:
                wl.append(("SEM_T", pB_tok))
            V(ctx, "tensor_mul", scores[:, e, :], sctmp[e % 2], B["pb"][3][:],
              waits=wl, evac=True)
            ctx.tok_ev["b3"] = ("E", ctx.c["E"])
            ctx.tok_ev["sct%d" % (e % 2)] = ("E", ctx.c["E"])
        for v in range(KD):
            key = "o%d%d" % (v % 2, (v % 4) // 2)
            po = B["pO"][v % 2]
            half = (v % 4) // 2
            pseg = po[:, 512 * half:512 * (half + 1)]
            ins = None
            if ctx.t():
                ctx.wait_tok(ctx.tok_ev.get(key, ("V", 0)))
                if v == 0:
                    ctx.wait("SEM_E", ctx.c["E"])
                for ee in range(16):
                    ins = ctx.eng.matmul(pseg, values_sb[:, ee, 128 * v:128 * (v + 1)],
                                         scores[:, ee, :],
                                         start=(ee == 0), stop=(ee == 15))
            ctx.tok_mm[key] = ctx.bump_t(ins)
            out = B["xr"][:, v, cols]
            V(ctx, "tensor_add", out, out, pseg,
              waits=[("SEM_T", ctx.tok_mm[key])], evac=True)
            ctx.tok_ev[key] = ("E", ctx.c["E"])
        pB_tok = ctx.c["T"]


def ph_comb_gates(ctx):
    B = ctx.B
    comb_mM = scr_f32(B, 0, [4, 2048])
    glT = scr_f32(B, 8, [128, 64])
    gexT = scr_f32(B, 9, [128, 64])
    gkT = scr_f32(B, 10, [128, 64])
    gw8 = scr_f32(B, 11, [128, 128])
    comb_tm = scr_f32(B, 13, [128, 16, 4])
    gl_raw = scr_f32(B, 14, [128, 64])

    if not ctx.t():
        ctx.join()
    for ts in range(16):
        ins = None
        if ctx.t():
            ctx.wait("SEM_S", ctx.ln_tok[ts // 4])
            if ts == 0:
                ctx.wait_tok(ctx.tok_ev.get("b3", ("V", 0)))
            for k in range(KD):
                ins = ctx.eng.matmul(B["pb"][3][:, 4 * ts:4 * (ts + 1)],
                                     B["xn"][:, k, 128 * ts:128 * (ts + 1)],
                                     B["gw_b"][:, k, :],
                                     start=(k == 0), stop=(k == KD - 1))
        ctx.tok_mm["b3"] = ctx.bump_t(ins)
    V(ctx, "tensor_copy", gl_raw, B["pb"][3][:, 0:64],
      waits=[("SEM_T", ctx.tok_mm["b3"])], evac=True)
    ctx.tok_ev["b3"] = ("E", ctx.c["E"])
    V(ctx, "tensor_copy", _view(glT, [128, 4, 16]),
      _view(gl_raw, [128, 16, 4]).rearrange("p a b -> p b a"))
    cj = [glT[:, 16 * j:16 * (j + 1)] for j in range(4)]
    wj = [gw8[:, 16 * j:16 * (j + 1)] for j in range(8)]
    V(ctx, "tensor_max", wj[0], cj[0], cj[1])
    V(ctx, "tensor_tensor", wj[1], cj[0], cj[1], ALU.min)
    V(ctx, "tensor_max", wj[2], cj[2], cj[3])
    V(ctx, "tensor_tensor", wj[3], cj[2], cj[3], ALU.min)
    V(ctx, "tensor_tensor", wj[4], wj[0], wj[2], ALU.min)
    V(ctx, "tensor_max", wj[5], wj[1], wj[3])
    V(ctx, "tensor_max", wj[6], wj[4], wj[5])
    SS(ctx, "activation", gexT, glT, AF.Exp, waits=[("SEM_V", ctx.c["V"])])
    for j in range(4):
        V(ctx, "tensor_tensor", gkT[:, 16 * j:16 * (j + 1)], cj[j], wj[6], ALU.is_ge,
          waits=([("SEM_S", ctx.c["S"])] if j == 0 else []))
    V(ctx, "tensor_mul", gexT, gexT, gkT)
    V(ctx, "tensor_add", wj[7], gexT[:, 0:16], gexT[:, 16:32])
    V(ctx, "tensor_add", wj[7], wj[7], gexT[:, 32:48])
    V(ctx, "tensor_add", wj[7], wj[7], gexT[:, 48:64])
    V(ctx, "reciprocal", wj[0], wj[7])
    for j in range(4):
        V(ctx, "tensor_mul",
          comb_tm[:, :, j:j + 1].rearrange("p a b -> p (a b)"),
          gexT[:, 16 * j:16 * (j + 1)], wj[0])
    for q in range(4):
        ins = None
        if ctx.t():
            ctx.wait_tok(ctx.tok_ev.get("b2", ("V", 0)))
            ctx.wait("SEM_V", ctx.c["V"])
            for r in range(4):
                ts = 4 * q + r
                ins = ctx.eng.transpose(B["pb"][2][0:4, 128 * r:128 * (r + 1)],
                                        comb_tm[:, ts, :], B["id_f"][:])
        ctx.tok_mm["b2"] = ctx.bump_t(ins)
        SS(ctx, "copy", comb_mM[:, 512 * q:512 * (q + 1)], B["pb"][2][0:4, :],
           waits=[("SEM_T", ctx.tok_mm["b2"])], evac=True)
        ctx.tok_ev["b2"] = ("E", ctx.c["E"])


def ph_attn1(ctx):
    B = ctx.B
    comb_mM = scr_f32(B, 0, [4, 2048])
    comb_b = scr_f32(B, 8, [128, TC])
    xsp = [scr_f32(B, 16, [128, TC]), scr_f32(B, 8, [128, TC])]
    a_t = scr_f32(B, 24, [128, TC])
    hP = [(scr_f32(B, 32, [128, TC]), scr_f32(B, 40, [128, TC])),
          (scr_f32(B, 48, [128, TC]), scr_f32(B, 56, [128, TC]))]
    sgq = scr_f32(B, 64, [128, 512])
    tvq = scr_f32(B, 66, [128, 512])

    slabs = [Slab(f"dSL{i}", ctx.c["T"]) for i in range(6)]
    sview = [wst_bf(B, 2 * i, [128, KD, 128]) for i in range(6)]

    def asrc(name, m, dt):
        return B[name][m][:, 128 * dt:128 * (dt + 1)].rearrange("(k p) f -> p k f", p=128)

    def load_mdt(j, m, dt):
        if ATTN1_NO_PROJ:
            return
        base = (j % 2) * 3
        for wi, name in enumerate(("wga", "wva", "wda")):
            slab_load(ctx, slabs[base + wi], sview[base + wi], asrc(name, m, dt))

    load_mdt(0, 0, 0)
    xs_free = [0, 0]
    tvq_tok = 0
    sgq_tok = 0

    def gen_comb_b(m, prev_use_tok):
        for q in range(4):
            ins = None
            if ctx.t():
                ctx.wait_tok(ctx.tok_ev.get("b2", ("V", 0)))
                ins = ctx.eng.matmul(B["pb"][2][:], B["e4f"][:, 128 * m:128 * (m + 1)],
                                     comb_mM[:, 512 * q:512 * (q + 1)],
                                     start=True, stop=True)
            ctx.tok_mm["b2"] = ctx.bump_t(ins)
            wl = [("SEM_T", ctx.tok_mm["b2"])]
            if q == 0:
                wl.append(("SEM_V", prev_use_tok))
            SS(ctx, "copy", comb_b[:, 512 * q:512 * (q + 1)], B["pb"][2][:], waits=wl,
               evac=True)
            ctx.tok_ev["b2"] = ("E", ctx.c["E"])

    for m in range(4):
        for dt in range(KD):
            j = m * KD + dt
            if j >= ATTN1_MDTS:
                continue
            base = (j % 2) * 3
            if j + 1 < min(32, ATTN1_MDTS):
                m2, dt2 = divmod(j + 1, KD)
                load_mdt(j + 1, m2, dt2)
            h_t, P_t = hP[j % 2]
            xs = xsp[j % 2]
            col = j
            for wi in range(0 if not ATTN1_NO_PROJ else 0, 3 if not ATTN1_NO_PROJ else 0):
                slot = slabs[base + wi]
                slab_wait(ctx, slot)
                for tt in range(4):
                    cols = slice(512 * tt, 512 * (tt + 1))
                    bank = "b%d" % tt
                    ins = None
                    if ctx.t():
                        ctx.wait_tok(ctx.tok_ev.get(bank, ("V", 0)))
                        for k in range(KD):
                            ins = ctx.eng.matmul(B["pb"][tt][:],
                                                 sview[base + wi][:, k, :],
                                                 B["xn"][:, k, cols],
                                                 start=(k == 0), stop=(k == KD - 1))
                    ctx.tok_mm[bank] = ctx.bump_t(ins)
                    if wi == 0:
                        wl = [("SEM_T", ctx.tok_mm[bank])]
                        if tt == 0:
                            wl.append(("SEM_V", xs_free[j % 2]))
                        SS(ctx, "activation", xs[:, cols], B["pb"][tt][:],
                           AF.Sigmoid, bias=B["bga_f"][:, col:col + 1], waits=wl,
                           evac=True)
                        ctx.tok_ev[bank] = ("E", ctx.c["E"])
                    elif wi == 1:
                        SS(ctx, "activation", tvq, B["pb"][tt][:],
                           AF.Tanh, bias=B["bva_f"][:, col:col + 1],
                           waits=[("SEM_T", ctx.tok_mm[bank]),
                                  ("SEM_V", tvq_tok)], evac=True)
                        ctx.tok_ev[bank] = ("E", ctx.c["E"])
                        tvq_tok = V(ctx, "tensor_mul", xs[:, cols], xs[:, cols], tvq,
                                    waits=[("SEM_E", ctx.c["E"])])
                    else:
                        SS(ctx, "activation", sgq, B["pb"][tt][:],
                           AF.Sigmoid, bias=B["bda_f"][:, col:col + 1],
                           waits=[("SEM_T", ctx.tok_mm[bank]),
                                  ("SEM_V", sgq_tok)], evac=True)
                        ctx.tok_ev[bank] = ("E", ctx.c["E"])
                        sgq_tok = V(ctx, "tensor_scalar", a_t[:, cols], sgq, 0.998, 0.001,
                                    ALU.mult, ALU.add, waits=[("SEM_E", ctx.c["E"])])
                slab_free(ctx, slot)
            if ATTN1_NO_SCAN:
                V(ctx, "tensor_copy", h_t, xs, waits=[("dSP", ctx.d.get("dSP", 0))])
                V(ctx, "tensor_copy", P_t, a_t)
            else:
                V(ctx, "tensor_tensor_scan", h_t, a_t, xs, 0.0, ALU.mult, ALU.add,
                  waits=[("dSP", ctx.d.get("dSP", 0))])
                V(ctx, "tensor_tensor_scan", P_t, a_t, a_t, 1.0, ALU.mult, ALU.bypass)
            xs_free[j % 2] = ctx.c["V"]
            V(ctx, "tensor_copy", B["exch"][:, j:j + 1], P_t[:, TC - 1:TC])
            V(ctx, "tensor_copy", B["exch"][:, 32 + j:32 + j + 1], h_t[:, TC - 1:TC])
            if not ATTN1_NO_SPILL:
                ins1 = ins2 = None
                if ctx.y():
                    ctx.wait("SEM_V", ctx.c["V"])
                    ins1 = ctx.eng.dma_start(out=B["sp_h"][j], in_=h_t)
                    ins2 = ctx.eng.dma_start(out=B["sp_p"][j], in_=P_t)
                ctx.dma("dSP", ins1)
                ctx.dma("dSP", ins2)

    ins = None
    if ctx.g():
        ctx.wait("SEM_V", ctx.c["V"])
        ins = ctx.eng.dma_start(out=B["cc_in"][:, :], in_=B["exch"][:])
    ctx.dma("dCC", ins)
    if not SKIP_CC:
        if ctx.g():
            ctx.wait("dCC", ctx.d["dCC"])
            ctx.eng.collective_compute(
                "AllGather", ALU.bypass, replica_groups=[list(range(NCORE))],
                ins=[B["cc_in"].ap().opt()], outs=[B["cc_out"].ap().opt()]
            ).then_inc(ctx.S["cCC"])
            ctx.wait("cCC", 1)
    else:
        for r in range(NCORE):
            ins = None
            if ctx.g():
                ctx.wait("dCC", ctx.d["dCC"])
                ins = ctx.eng.dma_start(out=B["cc_out"].ap()[128 * r:128 * (r + 1), :],
                                        in_=B["cc_in"][:, :])
            ctx.dma("dCC", ins)
    if not ATTN1_NO_AH:
        for r in range(NCORE):
            ins = None
            if ctx.g():
                ins = ctx.eng.dma_start(out=B["AH"][:, r, :],
                                        in_=B["cc_out"].ap()[128 * r:128 * (r + 1), :])
            ctx.dma("dCC", ins)
    if not ATTN1_NO_COMBINE:
        V(ctx, "memset", B["carry_f"][:], 0.0, waits=[("dCC", ctx.d["dCC"])])
        for r in range(NCORE - 1):
            V(ctx, "tensor_scalar", sgq[:, 0:32], B["AH"][:, r, 0:32],
              B["msel_f"][:, r:r + 1], B["mselinv_f"][:, r:r + 1], ALU.mult, ALU.add)
            V(ctx, "tensor_mul", B["carry_f"][:], B["carry_f"][:], sgq[:, 0:32])
            V(ctx, "tensor_scalar", sgq[:, 32:64], B["AH"][:, r, 32:64],
              B["msel_f"][:, r:r + 1], None, ALU.mult)
            V(ctx, "tensor_add", B["carry_f"][:], B["carry_f"][:], sgq[:, 32:64])

    if ATTN1_NO_PASS2:
        return
    rl_sem = ["dRL0", "dRL1"]
    use_tok = [0, 0]
    for m in range(4):
        gen_comb_b(m, ctx.c["V"])
        cb_tok = ctx.c["E"]
        for dt in range(KD):
            j = m * KD + dt
            if j >= ATTN1_MDTS:
                continue
            h_t, P_t = hP[j % 2]
            sem = rl_sem[j % 2]
            ins1 = ins2 = None
            if ctx.y():
                if use_tok[j % 2]:
                    ctx.wait("SEM_V", use_tok[j % 2])
                else:
                    ctx.wait("dSP", ctx.d["dSP"])
                ins1 = ctx.eng.dma_start(out=h_t, in_=B["sp_h"][j])
                ins2 = ctx.eng.dma_start(out=P_t, in_=B["sp_p"][j])
            ctx.dma(sem, ins1)
            ctx.dma(sem, ins2)
            V(ctx, "scalar_tensor_tensor", h_t, P_t, B["carry_f"][:, j:j + 1], h_t,
              ALU.mult, ALU.add,
              waits=[(sem, ctx.d[sem]), ("SEM_E", cb_tok)])
            V(ctx, "tensor_mul", h_t, h_t, comb_b)
            out = B["xr"][:, dt, :]
            V(ctx, "tensor_add", out, out, h_t)
            use_tok[j % 2] = ctx.c["V"]


def ph_mask_gates(ctx):
    B = ctx.B
    mask_mM = scr_bf(B, 64, [8, 2048])
    glT = scr_f32(B, 0, [128, 128])
    gl_raw = scr_f32(B, 1, [128, 128])
    mw = scr_f32(B, 2, [128, 128])
    mres = scr_f32(B, 4, [128, 16, 8])

    ctx.join()
    for ts in range(16):
        ins = None
        if ctx.t():
            if ts == 0:
                ctx.wait_tok(ctx.tok_ev.get("b3", ("V", 0)))
            for k in range(KD):
                ins = ctx.eng.matmul(B["pb"][3][:, 8 * ts:8 * (ts + 1)],
                                     B["xn"][:, k, 128 * ts:128 * (ts + 1)],
                                     B["gfw_b"][:, k, :],
                                     start=(k == 0), stop=(k == KD - 1))
        ctx.tok_mm["b3"] = ctx.bump_t(ins)
    V(ctx, "tensor_copy", gl_raw, B["pb"][3][:, 0:128],
      waits=[("SEM_T", ctx.tok_mm["b3"])], evac=True)
    for ts in range(16):
        V(ctx, "tensor_add", _view(gl_raw, [128, 16, 8])[:, ts, :],
          _view(gl_raw, [128, 16, 8])[:, ts, :], B["bias8_b"][:])
    ctx.tok_ev["b3"] = ("E", ctx.c["E"])
    V(ctx, "tensor_copy", _view(glT, [128, 8, 16]),
      _view(gl_raw, [128, 16, 8]).rearrange("p a b -> p b a"))
    cj = [glT[:, 16 * j:16 * (j + 1)] for j in range(8)]
    wj = [mw[:, 16 * j:16 * (j + 1)] for j in range(8)]
    V(ctx, "tensor_max", wj[0], cj[0], cj[1])
    V(ctx, "tensor_max", wj[1], cj[2], cj[3])
    V(ctx, "tensor_max", wj[2], cj[4], cj[5])
    V(ctx, "tensor_max", wj[3], cj[6], cj[7])
    V(ctx, "tensor_max", wj[4], wj[0], wj[1])
    V(ctx, "tensor_max", wj[5], wj[2], wj[3])
    V(ctx, "tensor_max", wj[6], wj[4], wj[5])
    for j in range(8):
        V(ctx, "tensor_tensor",
          mres[:, :, j:j + 1].rearrange("p a b -> p (a b)"),
          cj[j], wj[6], ALU.is_ge)
    for q in range(4):
        ins = None
        if ctx.t():
            ctx.wait_tok(ctx.tok_ev.get("b2", ("V", 0)))
            ctx.wait("SEM_V", ctx.c["V"])
            for r in range(4):
                ts = 4 * q + r
                ins = ctx.eng.transpose(B["pb"][2][0:8, 128 * r:128 * (r + 1)],
                                        mres[:, ts, :], B["id_f"][:])
        ctx.tok_mm["b2"] = ctx.bump_t(ins)
        SS(ctx, "copy", mask_mM[:, 512 * q:512 * (q + 1)], B["pb"][2][0:8, :],
           waits=[("SEM_T", ctx.tok_mm["b2"])], evac=True)
        ctx.tok_ev["b2"] = ("E", ctx.c["E"])


def ph_ffn1(ctx):
    B = ctx.B
    h1 = scr_bf(B, 0, [128, 32, 1024])
    mask_mM = scr_bf(B, 64, [8, 2048])
    mask_b = scr_bf(B, 68, [128, 2, 512])
    tbuf = [scr_f32(B, min(70, SCR_KB - 4), [128, 512]), scr_f32(B, min(72, SCR_KB - 2), [128, 512])]

    w1slabs = [Slab("dSL0", ctx.c["T"]), Slab("dSL1", ctx.c["T"])]
    w1view = [wst_bf(B, 4 * i, [128, KD, 256]) for i in range(2)]
    w2slabs = [Slab("dSL2", ctx.c["T"]), Slab("dSL3", ctx.c["T"])]
    w2view = [wst_bf(B, 8 + 8 * i, [128, 32, 128]) for i in range(2)]

    def w1src(e, fq):
        return B["w1"][e][:, 256 * fq:256 * (fq + 1)].rearrange("(k p) f -> p k f", p=128)

    def w2src(e, v):
        return B["w2"][e][:, 128 * v:128 * (v + 1)].rearrange("(k p) v -> p k v", p=128)

    w1seq = [(e, fq) for th in range(2) for e in range(8) for fq in range(16)]
    w2seq = [(e, v) for th in range(2) for e in range(8) for v in range(KD)]

    slab_load(ctx, w1slabs[0], w1view[0], w1src(*w1seq[0]))
    n1 = 1
    slab_load(ctx, w2slabs[0], w2view[0], w2src(*w2seq[0]))
    n2 = 1

    y_tok = 0
    for th in range(2):
        for e in range(8):
            for tt in range(2):
                ins = None
                if ctx.t():
                    ctx.wait_tok(ctx.tok_ev.get("b2", ("V", 0)))
                    ins = ctx.eng.matmul(
                        B["pb"][2][:], B["ematb"][0:8, 128 * e:128 * (e + 1)],
                        mask_mM[:, 1024 * th + 512 * tt:1024 * th + 512 * (tt + 1)],
                        start=True, stop=True)
                ctx.tok_mm["b2"] = ctx.bump_t(ins)
                wl = [("SEM_T", ctx.tok_mm["b2"])]
                if tt == 0:
                    wl.append(("SEM_E", ctx.c["E"]))
                SS(ctx, "copy", mask_b[:, tt, :], B["pb"][2][:], waits=wl, evac=True)
                ctx.tok_ev["b2"] = ("E", ctx.c["E"])
            for fq in range(16):
                j1 = th * 128 + e * 16 + fq
                slot = w1slabs[j1 % 2]
                if n1 < len(w1seq):
                    slab_load(ctx, w1slabs[n1 % 2], w1view[n1 % 2], w1src(*w1seq[n1]))
                    n1 += 1
                slab_wait(ctx, slot)
                for f2 in range(2):
                    ft = fq * 2 + f2
                    for tt in range(2):
                        bank = "b%d" % (2 * f2 + tt)
                        ins = None
                        if ctx.t():
                            ctx.wait_tok(ctx.tok_ev.get(bank, ("V", 0)))
                            for k in range(KD):
                                ins = ctx.eng.matmul(
                                    B["pb"][2 * f2 + tt][:],
                                    w1view[j1 % 2][:, k, 128 * f2:128 * (f2 + 1)],
                                    B["xn"][:, k, 1024 * th + 512 * tt:
                                            1024 * th + 512 * (tt + 1)],
                                    start=(k == 0), stop=(k == KD - 1))
                        ctx.tok_mm[bank] = ctx.bump_t(ins)
                        bcol = e * 32 + ft
                        first = (fq == 0 and f2 == 0 and tt == 0)
                        if ft % 2 == 0:
                            wl = [("SEM_T", max(ctx.tok_mm[bank], y_tok if first else 0))]
                            SS(ctx, "activation", h1[:, ft, 512 * tt:512 * (tt + 1)],
                               B["pb"][2 * f2 + tt][:], AF.Relu,
                               bias=B["b1c_f"][:, bcol:bcol + 1], waits=wl, evac=True)
                            ctx.tok_ev[bank] = ("E", ctx.c["E"])
                        else:
                            wl = [("SEM_T", max(ctx.tok_mm[bank], y_tok if fq == 0 else 0))]
                            V(ctx, "tensor_scalar", h1[:, ft, 512 * tt:512 * (tt + 1)],
                              B["pb"][2 * f2 + tt][:], B["b1c_f"][:, bcol:bcol + 1], 0.0,
                              ALU.add, ALU.max, waits=wl, evac=True)
                            ctx.tok_ev[bank] = ("E", ctx.c["E"])
                slab_free(ctx, slot)
            h1_tok_e = ctx.c["E"]
            for v in range(KD):
                j2 = th * 64 + e * 8 + v
                slot = w2slabs[j2 % 2]
                if n2 < len(w2seq):
                    slab_load(ctx, w2slabs[n2 % 2], w2view[n2 % 2], w2src(*w2seq[n2]))
                    n2 += 1
                slab_wait(ctx, slot)
                for tt in range(2):
                    bank = "b%d" % (2 + tt)
                    pbank = B["pb"][2 + tt]
                    ins = None
                    if ctx.t():
                        ctx.wait_tok(ctx.tok_ev.get(bank, ("V", 0)))
                        if v == 0 and tt == 0:
                            ctx.wait("SEM_E", h1_tok_e)
                        for fc in range(32):
                            ins = ctx.eng.matmul(pbank[:],
                                                 w2view[j2 % 2][:, fc, :],
                                                 h1[:, fc, 512 * tt:512 * (tt + 1)],
                                                 start=(fc == 0), stop=(fc == 31))
                    ctx.tok_mm[bank] = ctx.bump_t(ins)
                    bcol = e * 8 + v
                    V(ctx, "scalar_tensor_tensor", tbuf[tt], pbank[:],
                      B["b2c_f"][:, bcol:bcol + 1], mask_b[:, tt, :],
                      ALU.add, ALU.mult,
                      waits=[("SEM_T", ctx.tok_mm[bank]), ("SEM_E", ctx.c["E"])])
                    out = B["xr"][:, v, 1024 * th + 512 * tt:
                                  1024 * th + 512 * (tt + 1)]
                    V(ctx, "tensor_add", out, out, tbuf[tt], evac=True)
                    ctx.tok_ev[bank] = ("E", ctx.c["E"])
                slab_free(ctx, slot)
            y_tok = ctx.c["T"]


def ph_store(ctx):
    B = ctx.B
    for i in range(KD):
        ins = None
        if ctx.y():
            if i == 0:
                ctx.wait("SEM_V", ctx.c["V"])
                ctx.wait("SEM_S", ctx.c["S"])
                ctx.wait("SEM_E", ctx.c["E"])
                ctx.wait("SEM_T", ctx.c["T"])
            ins = ctx.eng.dma_start(out=B["y"][128 * i:128 * (i + 1), :],
                                    in_=B["xr"][:, i, :])
        ctx.dma("dST", ins)
    if ctx.y():
        ctx.wait("dST", ctx.d["dST"])


def emit_program(ctx, layers):
    ph_preload(ctx, layers)
    if 0 in layers:
        if "ln0" not in SKIP_PHASES:
            ph_ln(ctx, add_resid=True)
        if "attn0" not in SKIP_PHASES:
            ph_attn0(ctx)
        if "ffn0" not in SKIP_PHASES:
            ph_ffn0(ctx)
    if 1 in layers:
        if "ln1" not in SKIP_PHASES:
            ph_ln(ctx, add_resid=False)
        if "comb" not in SKIP_PHASES:
            ph_comb_gates(ctx)
        if "attn1" not in SKIP_PHASES:
            ph_attn1(ctx)
        if "mask" not in SKIP_PHASES:
            ph_mask_gates(ctx)
        if "ffn1" not in SKIP_PHASES:
            ph_ffn1(ctx)
    ph_store(ctx)



def host_consts(layers):
    c = {}
    c["cones"] = np.ones((128, 128), np.float32)
    c["cident"] = np.eye(128, dtype=np.float32)
    em = np.zeros((16, 2048), np.float32)
    for k in range(16):
        em[k, 128 * k:128 * (k + 1)] = 1.0
    c["cematb"] = em
    if 0 in layers:
        t = np.arange(128)
        tp = t[:, None]
        tq = t[None, :]
        same = (tq // 32) == (tp // 32)
        c["cdiff"] = np.where(same, tq % 32 - tp % 32, 0).astype(np.float32)
        c["clt"] = (same & ((tq % 32) >= (tp % 32))).astype(np.float32)
    if 1 in layers:
        e4 = np.zeros((4, 512), np.float32)
        for k in range(4):
            e4[k, 128 * k:128 * (k + 1)] = 1.0
        c["ce4"] = e4
    return c


def kernel(**inp):
    layers_list = LAYERS_LIST or ([(0, 1)] if FUSED else [(0,), (1,)])
    x = np.asarray(inp["L0_attn_x"])[0]
    x_sh = [np.ascontiguousarray(x[TC * cc:TC * (cc + 1), :].T) for cc in range(NCORE)]

    base0 = {
        "wq": np.ascontiguousarray(np.asarray(inp["L0_attn_W_q"]).T),
        "wk": np.ascontiguousarray(np.asarray(inp["L0_attn_W_k"]).T),
        "wv": np.ascontiguousarray(np.asarray(inp["L0_attn_W_v"]).T),
        "wg": np.ascontiguousarray(np.asarray(inp["L0_attn_W_g"]).T),
        "wo": np.ascontiguousarray(np.asarray(inp["L0_attn_W_o"]).T),
        "ld0": np.asarray(inp["L0_attn_log_decay"])[0:1].reshape(1, 1).astype(np.float32),
        "esel": np.ascontiguousarray(np.asarray(inp["L0_ffn_expert_sel"]).T),
        "keys": np.ascontiguousarray(np.asarray(inp["L0_ffn_keys"])),
        "values": np.ascontiguousarray(np.asarray(inp["L0_ffn_values"])),
    }
    bg = np.asarray(inp["L1_attn_bg"])
    bv = np.asarray(inp["L1_attn_bv"])
    bd = np.asarray(inp["L1_attn_bd"])
    b1 = np.asarray(inp["L1_ffn_b1"])
    b2 = np.asarray(inp["L1_ffn_b2"])
    base1 = {
        "wga": np.ascontiguousarray(np.asarray(inp["L1_attn_Wg"])),
        "wva": np.ascontiguousarray(np.asarray(inp["L1_attn_Wv"])),
        "wda": np.ascontiguousarray(np.asarray(inp["L1_attn_Wd"])),
        "bga": np.ascontiguousarray(bg.reshape(4, 8, 128).transpose(2, 0, 1).reshape(128, 32)),
        "bva": np.ascontiguousarray(bv.reshape(4, 8, 128).transpose(2, 0, 1).reshape(128, 32)),
        "bda": np.ascontiguousarray(bd.reshape(4, 8, 128).transpose(2, 0, 1).reshape(128, 32)),
        "gw": np.ascontiguousarray(np.asarray(inp["L1_attn_gate_W"])),
        "gfw": np.ascontiguousarray(np.asarray(inp["L1_ffn_gate_w"]).T),
        "gfb": np.asarray(inp["L1_ffn_gate_b"]).reshape(1, 8).astype(np.float32),
        "efb": np.asarray(inp["L1_ffn_expert_bias"]).reshape(1, 8).astype(np.float32),
        "w1": np.ascontiguousarray(np.asarray(inp["L1_ffn_w1"])),
        "b1c": np.ascontiguousarray(b1.reshape(8, 32, 128).transpose(2, 0, 1).reshape(128, 256)),
        "w2": np.ascontiguousarray(np.asarray(inp["L1_ffn_w2"])),
        "b2c": np.ascontiguousarray(b2.reshape(8, 8, 128).transpose(2, 0, 1).reshape(128, 64)),
    }

    cur = x_sh
    for layers in layers_list:
        consts = host_consts(layers)
        nc = build_program(layers)
        in_maps = []
        for cc in range(NCORE):
            m = dict(consts)
            m["x"] = np.ascontiguousarray(cur[cc].astype(np.float32))
            if 0 in layers:
                m.update(base0)
            if 1 in layers:
                m.update(base1)
                msel = np.zeros((128, 8), np.float32)
                msel[:, :cc] = 1.0
                m["cmsel"] = msel
                m["cmselinv"] = np.ascontiguousarray(1.0 - msel)
            in_maps.append(m)
        res = run_bass_kernel_spmd(nc, in_maps, core_ids=list(range(NCORE)))
        cur = [res.results[cc]["y"] for cc in range(NCORE)]

    out = np.concatenate([s.T for s in cur], axis=0)[None]
    return np.ascontiguousarray(out.astype(np.float32))
